# version 09
# baseline (speedup 1.0000x reference)
import os
import numpy as np
from contextlib import ExitStack

KDBG = os.environ.get("KDBG") == "1"
KDBG2 = os.environ.get("KDBG2") == "1"
KDBG3 = os.environ.get("KDBG3") == "1"


import concourse.bass as bass
import concourse.mybir as mybir
from concourse.bass_utils import run_bass_kernel_spmd

F32 = mybir.dt.float32
BF16 = mybir.dt.bfloat16
AF = mybir.ActivationFunctionType
ALU = mybir.AluOpType
AX = mybir.AxisListType

B, L, D = 4, 4096, 1024
H, DH = 16, 64
E, HF = 8, 2816
NHFT = 22
VIEW, SEG, NSEG, CPS, DC = 4096, 1024, 4, 8, 8

XNT_O = 0
S16_O = 8192
KTM_O = 16384
VTM_O = 24576
KT_O = 32768
QT_O = 40960
PW_O = 49152
AB_COLS = 73728
H_O = 8192
W13_O = 32768
W2_O = 49152
MSC_O = S16_O
STG_O = S16_O + 2048

Y_O = 0
XR_O = 8192
GS_O = 10240
SLD_O = 11264
ASF_O = 13312
AF_COLS = 14336

GRP = [(0, 4), (4, 4), (8, 4), (12, 4), (16, 4), (20, 2)]


class Pass:
    def __init__(self, nc, which, sems, shared_marks):
        self.nc = nc
        self.which = which
        self.sems = sems
        self.cnt = {k: 0 for k in sems}
        self.marks = shared_marks
        self.lastw = {}
        self.guard = {}

    def on(self, eng, fn, inc=None, n=1):
        if eng == 'vector':
            self.w('vector', 'vp', self.cnt['vp'])
            if inc is None:
                inc, n = 'vp', 1
        if self.which == eng:
            inst = fn(getattr(self.nc, eng))
            if inc is not None:
                inst.then_inc(self.sems[inc], n)
        if inc is not None:
            self.cnt[inc] += n

    def w(self, eng, sem, val):
        if val <= 0:
            return
        key = (eng, sem)
        if self.lastw.get(key, -1) >= val:
            return
        self.lastw[key] = val
        if self.which == eng:
            getattr(self.nc, eng).wait_ge(self.sems[sem], val)

    def mark(self, key, sem):
        v = (sem, self.cnt[sem])
        if key in self.marks:
            assert self.marks[key] == v, (key, self.marks[key], v, self.which)
        else:
            self.marks[key] = v
        return v

    def wmark(self, eng, key):
        sem, v = self.marks[key]
        self.w(eng, sem, v)

    def wguard(self, eng, *res):
        for r in res:
            g = self.guard.get(r)
            if g is not None:
                self.w(eng, g[0], g[1])

    def setguard(self, res, sem):
        self.guard[res] = (sem, self.cnt[sem])


def emit(P, A):
    ab, af = A["ab"], A["af"]
    G, SP, Gb, sm = A["G"], A["SP"], A["Gb"], A["sm"]

    def pw(i, off, n):
        return ab[:, PW_O + i * 8192 + off: PW_O + i * 8192 + off + n]

    def packed_dst(base, j, r=1024):
        st = base + j * 128
        return ab[:, st: st + 4 * r].rearrange("p (c r) -> p c r", r=r)[:, :, 0:128]

    P.on('gpsimd', lambda g: g.dma_start(out=sm['ident'][:], in_=A['d_ident']),
         inc='w', n=16)
    P.mark('ident', 'w')
    P.on('gpsimd', lambda g: g.dma_start(
        out=sm['wb'][:].rearrange("p (c f) -> p c f", c=DC),
        in_=A['d_wb'].rearrange("(c p) f -> p c f", p=128)), inc='w', n=16)
    P.mark('wb', 'w')
    P.on('gpsimd', lambda g: g.dma_start(
        out=sm['wg'][:].rearrange("p (c f) -> p c f", c=DC),
        in_=A['d_wgt'].rearrange("(c p) f -> p c f", p=128)), inc='w', n=16)
    P.mark('wg', 'w')
    P.on('sync', lambda s: s.dma_start(out=sm['maskt'][:], in_=A['d_maskt']),
         inc='x', n=16)
    P.mark('maskt', 'x')
    P.on('sync', lambda s: s.dma_start(out=sm['bbb'][:], in_=A['d_bbb']),
         inc='x', n=16)
    P.mark('bbb', 'x')

    nseg_emit = int(os.environ.get("KSEGS", str(NSEG)))
    for seg in range(nseg_emit):
        main = seg >= 2
        tok0 = seg * SEG

        if seg in (0, 3):
            P.wguard('gpsimd', 'pw1')
            P.on('gpsimd', lambda g: g.dma_start(
                out=pw(1, 0, 8192).rearrange("p (c f) -> p c f", c=DC),
                in_=A['d_wk'].rearrange("(c p) f -> p c f", p=128)), inc='w', n=16)
            P.mark(('wk', seg), 'w')
            P.wguard('gpsimd', 'pw2')
            P.on('gpsimd', lambda g: g.dma_start(
                out=pw(2, 0, 8192).rearrange("p (c f) -> p c f", c=DC),
                in_=A['d_wv'].rearrange("(c p) f -> p c f", p=128)), inc='w', n=16)
            P.mark(('wv', seg), 'w')
        if main:
            P.wguard('gpsimd', 'pw0')
            P.on('gpsimd', lambda g: g.dma_start(
                out=pw(0, 0, 8192).rearrange("p (c f) -> p c f", c=DC),
                in_=A['d_wq'].rearrange("(c p) f -> p c f", p=128)), inc='w', n=16)
            P.mark(('wq', seg), 'w')

        for j in range(CPS):
            xr = af[:, XR_O + (j % 2) * 1024: XR_O + (j % 2) * 1024 + 1024]
            P.wguard('sync', f'xr{j % 2}')
            P.on('sync', lambda s: s.dma_start(
                out=xr, in_=A['d_xseq'][tok0 + j * 128: tok0 + (j + 1) * 128, :]),
                inc='x', n=16)
            P.mark(('lnx', seg, j), 'x')

            st = sm['stats']
            c0 = (j % 2) * 16
            msum, mneg = st[:, c0:c0 + 1], st[:, c0 + 2:c0 + 3]
            ssum, rstd = st[:, c0 + 1:c0 + 2], st[:, c0 + 3:c0 + 4]
            ex2, m2 = st[:, c0 + 4:c0 + 5], st[:, c0 + 5:c0 + 6]
            var, sd = st[:, c0 + 6:c0 + 7], st[:, c0 + 7:c0 + 8]
            P.wmark('vector', ('lnx', seg, j))
            P.wmark('scalar', ('lnx', seg, j))
            P.on('vector', lambda v: v.reduce_sum(msum, xr, axis=AX.X))
            P.on('vector', lambda v: v.tensor_scalar_mul(mneg, msum, -1.0 / 1024.0))
            sqd = af[:, GS_O:GS_O + 1024]
            P.wguard('scalar', 'gs')
            P.on('scalar', lambda s: s.activation(sqd, xr, AF.Square, accum_out=ssum),
                 inc='sp')
            P.mark(('lnsq', seg, j), 'sp')
            P.wmark('vector', ('lnsq', seg, j))
            P.on('vector', lambda v: v.tensor_scalar_mul(ex2, ssum, 1.0 / 1024.0))
            P.on('vector', lambda v: v.tensor_mul(m2, mneg, mneg))
            P.on('vector', lambda v: v.tensor_sub(var, ex2, m2))
            P.on('vector', lambda v: v.tensor_scalar_add(var, var, 1e-5), inc='vp')
            P.mark(('lnvar', seg, j), 'vp')
            P.wmark('scalar', ('lnvar', seg, j))
            P.on('scalar', lambda s: s.activation(sd, var, AF.Sqrt),
                 inc='sp')
            P.mark(('lnsd', seg, j), 'sp')
            P.wmark('vector', ('lnsd', seg, j))
            P.on('vector', lambda v: v.reciprocal(rstd, sd))
            xn = ab[:, S16_O + j * 1024: S16_O + (j + 1) * 1024]
            P.on('vector', lambda v: v.tensor_scalar(xn, xr, mneg, rstd,
                                                     ALU.add, ALU.mult), inc='vp')
            P.mark(('lnap', seg, j), 'vp')
            P.setguard(f'xr{j % 2}', 'vp')

        P.wmark('tensor', 'ident')
        for j in range(CPS):
            P.wmark('tensor', ('lnap', seg, j))
            for fg2 in range(2):
                bank = (j * 2 + fg2) % 2
                P.wguard('tensor', f'B{bank}')
                for k in range(4):
                    fc = fg2 * 4 + k
                    src = ab[:, S16_O + j * 1024 + fc * 128:
                             S16_O + j * 1024 + fc * 128 + 128]
                    P.on('tensor', lambda t, k=k, bank=bank, src=src:
                         t.transpose(Gb[bank][:, k * 128:(k + 1) * 128], src,
                                     sm['ident'][:]),
                         )
                P.on('tensor', lambda t: t.drain(), inc='tp')
                P.mark(('xt', seg, j, fg2), 'tp')
                P.wmark('scalar', ('xt', seg, j, fg2))
                dst = packed_dst(XNT_O + fg2 * 4 * 1024, j)
                P.on('scalar', lambda s, bank=bank, dst=dst:
                     s.copy(dst, Gb[bank][:, 0:512].rearrange("p (c r) -> p c r", c=4)),
                     inc='sp')
                P.mark(('xnT', seg, j, fg2), 'sp')
                P.setguard(f'B{bank}', 'sp')
        P.mark(('xnT-done', seg), 'sp')

        if os.environ.get("KPH") == "ln":
            continue
        P.wmark('tensor', 'wb')
        if main:
            P.wmark('tensor', 'wg')
        P.wguard('tensor', 'B2')
        for j in range(CPS):
            P.wmark('tensor', ('xnT', seg, j, 0))
            P.wmark('tensor', ('xnT', seg, j, 1))
            for c in range(DC):
                xt = ab[:, XNT_O + c * 1024 + j * 128: XNT_O + c * 1024 + j * 128 + 128]
                first = (j == 0 and c == 0)
                last = (c == DC - 1)
                if main:
                    P.on('tensor', lambda t, xt=xt, j=j, first=first, last=last:
                         t.matmul(G[2][:, j * 24: j * 24 + 16], xt,
                                  sm['wb'][:, c * 16:(c + 1) * 16],
                                  start=first, stop=last))
                    P.on('tensor', lambda t, xt=xt, j=j, last=last:
                         t.matmul(G[2][:, j * 24 + 16: j * 24 + 24], xt,
                                  sm['wg'][:, c * 8:(c + 1) * 8],
                                  start=False, stop=last))
                else:
                    P.on('tensor', lambda t, xt=xt, j=j, first=first, last=last:
                         t.matmul(G[2][:, j * 24: j * 24 + 16], xt,
                                  sm['wb'][:, c * 16:(c + 1) * 16],
                                  start=first, stop=last))
            P.on('tensor', lambda t: t.drain(), inc='tp')
            P.mark(('pre', seg, j), 'tp')
        P.wmark('vector', ('pre', seg, CPS - 1))
        P.wmark('vector', 'bbb')
        bsl = G[2][:, 0:192].rearrange("p (j f) -> p j f", j=8)[:, :, 0:16]
        bbb_b = sm['bbb'][:].unsqueeze(1).broadcast_to([128, 8, 16])
        P.on('vector', lambda v: v.tensor_tensor(
            sm['btmp'][:].rearrange("p (j f) -> p j f", j=8), bsl, bbb_b, ALU.add),
            inc='vp')
        P.mark(('btmp', seg), 'vp')
        P.wmark('scalar', ('btmp', seg))
        P.on('scalar', lambda s: s.activation(sm['bseg'][:], sm['btmp'][:], AF.Sigmoid),
             inc='sp')
        P.mark(('bseg', seg), 'sp')
        if main:
            for j in range(CPS):
                r = sm['rtr']
                mx, nmx, zz, rz = r[:, 0:1], r[:, 1:2], r[:, 2:3], r[:, 3:4]
                m1, m2r, den, rd = r[:, 4:5], r[:, 5:6], r[:, 6:7], r[:, 7:8]
                ex, sc, sc2 = sm['ex'][:, 0:8], sm['ex'][:, 8:16], sm['ex'][:, 16:24]
                e1, msk, cwr = sm['ex'][:, 24:32], sm['ex'][:, 32:40], sm['ex'][:, 40:48]
                lg = G[2][:, j * 24 + 16: j * 24 + 24]
                P.on('vector', lambda v: v.reduce_max(mx, lg, axis=AX.X))
                P.on('vector', lambda v: v.tensor_scalar_mul(nmx, mx, -1.0))
                P.on('vector', lambda v: v.tensor_scalar_add(sc2, lg, nmx), inc='vp')
                P.mark(('rsub', seg, j), 'vp')
                P.wmark('scalar', ('rsub', seg, j))
                P.on('scalar', lambda s: s.activation(ex, sc2, AF.Exp), inc='sp')
                P.mark(('rexp', seg, j), 'sp')
                P.wmark('vector', ('rexp', seg, j))
                P.on('vector', lambda v: v.reduce_sum(zz, ex, axis=AX.X))
                P.on('vector', lambda v: v.reciprocal(rz, zz))
                P.on('vector', lambda v: v.tensor_scalar_mul(sc, ex, rz))
                P.on('vector', lambda v: v.reduce_max(m1, sc, axis=AX.X))
                P.on('vector', lambda v: v.tensor_single_scalar(e1, sc, m1, ALU.is_ge))
                P.on('vector', lambda v: v.scalar_tensor_tensor(sc2, e1, -1e30, sc,
                                                                ALU.mult, ALU.add))
                P.on('vector', lambda v: v.reduce_max(m2r, sc2, axis=AX.X))
                P.on('vector', lambda v: v.tensor_single_scalar(msk, sc, m2r, ALU.is_ge))
                P.on('vector', lambda v: v.tensor_mul(cwr, sc, msk))
                P.on('vector', lambda v: v.reduce_sum(den, cwr, axis=AX.X))
                P.on('vector', lambda v: v.tensor_scalar_add(den, den, 1e-20))
                P.on('vector', lambda v: v.reciprocal(rd, den))
                P.on('vector', lambda v, j=j: v.tensor_scalar_mul(
                    sm['cw'][:, j * 8:(j + 1) * 8], cwr, rd), inc='vp')
                P.mark(('cw', seg, j), 'vp')
        P.setguard('B2', 'vp')
        P.mark(('pre-done', seg), 'vp')

        if os.environ.get("KPH") == "pre":
            continue
        P.wmark('tensor', ('xnT-done', seg))
        P.wmark('tensor', ('pre-done', seg))
        P.wmark('tensor', ('wk', 0 if seg < 3 else 3))
        P.wmark('tensor', ('wv', 0 if seg < 3 else 3))
        if main:
            P.wmark('tensor', ('wq', seg))
        for j in range(CPS):
            for fg in range(2):
                bq, bk, bv = (0, 1, 2) if fg == 0 else (3, 4, 5)
                if main:
                    P.wguard('tensor', f'B{bq}')
                P.wguard('tensor', f'B{bk}', f'B{bv}')
                for c in range(DC):
                    xt = ab[:, XNT_O + c * 1024 + j * 128:
                            XNT_O + c * 1024 + j * 128 + 128]
                    st_, sp_ = (c == 0), (c == DC - 1)
                    if main:
                        P.on('tensor', lambda t, xt=xt, c=c, fg=fg, bq=bq, st_=st_, sp_=sp_:
                             t.matmul(G[bq][:], xt, pw(0, c * 1024 + fg * 512, 512),
                                      start=st_, stop=sp_))
                    P.on('tensor', lambda t, xt=xt, c=c, fg=fg, bk=bk, st_=st_, sp_=sp_:
                         t.matmul(G[bk][:], xt, pw(1, c * 1024 + fg * 512, 512),
                                  start=st_, stop=sp_))
                    P.on('tensor', lambda t, xt=xt, c=c, fg=fg, bv=bv, st_=st_, sp_=sp_:
                         t.matmul(G[bv][:], xt, pw(2, c * 1024 + fg * 512, 512),
                                  start=st_, stop=sp_))
                P.on('tensor', lambda t: t.drain(), inc='tp')
                P.mark(('projmm', seg, j, fg), 'tp')

                q2 = sm['qn']
                cb = fg * 32
                kss, ksd = q2[:, cb:cb + 8], q2[:, cb + 8:cb + 16]
                krc, rkb = q2[:, cb + 16:cb + 24], q2[:, cb + 24:cb + 32]
                cb2 = 64 + fg * 24
                qss, qsd, qrc = (q2[:, cb2:cb2 + 8], q2[:, cb2 + 8:cb2 + 16],
                                 q2[:, cb2 + 16:cb2 + 24])
                ksq = af[:, ASF_O: ASF_O + 512]
                qsq = af[:, ASF_O + 512: ASF_O + 1024]
                P.wmark('scalar', ('projmm', seg, j, fg))
                P.wguard('scalar', 'asf')
                if main:
                    P.on('scalar', lambda s: s.activation(ksq, G[bk][:], AF.Square),
                         inc='sp')
                    P.on('scalar', lambda s: s.activation(qsq, G[bq][:], AF.Square),
                         inc='sp')
                else:
                    P.on('scalar', lambda s: s.activation(ksq, G[bk][:], AF.Square),
                         inc='sp')
                P.mark(('projsq', seg, j, fg), 'sp')
                P.wmark('vector', ('projsq', seg, j, fg))
                P.wmark('vector', ('bseg', seg))
                if ('projsd', seg, j - 1, fg) in P.marks:
                    P.wmark('vector', ('projsd', seg, j - 1, fg))
                kview = ksq.rearrange("p (h w) -> p h w", w=64)
                if main:
                    P.on('vector', lambda v: v.reduce_sum(kss.unsqueeze(2), kview,
                                                          axis=AX.X))
                    qview = qsq.rearrange("p (h w) -> p h w", w=64)
                    P.on('vector', lambda v: v.reduce_sum(qss.unsqueeze(2), qview,
                                                          axis=AX.X), inc='vp')
                else:
                    P.on('vector', lambda v: v.reduce_sum(kss.unsqueeze(2), kview,
                                                          axis=AX.X), inc='vp')
                P.mark(('projss', seg, j, fg), 'vp')
                P.setguard('asf', 'vp')
                P.wmark('scalar', ('projss', seg, j, fg))
                if main:
                    P.on('scalar', lambda s: s.activation(ksd, kss, AF.Sqrt))
                    P.on('scalar', lambda s: s.activation(qsd, qss, AF.Sqrt), inc='sp')
                else:
                    P.on('scalar', lambda s: s.activation(ksd, kss, AF.Sqrt), inc='sp')
                P.mark(('projsd', seg, j, fg), 'sp')
                P.wmark('vector', ('projsd', seg, j, fg))
                P.on('vector', lambda v: v.tensor_scalar_max(ksd, ksd, 1e-12))
                P.on('vector', lambda v: v.reciprocal(krc, ksd))
                bslice = sm['bseg'][:, j * 16 + fg * 8: j * 16 + fg * 8 + 8]
                P.on('vector', lambda v: v.tensor_mul(rkb, krc, bslice))
                ktm_dst = ab[:, KTM_O + j * 1024 + fg * 512:
                             KTM_O + j * 1024 + fg * 512 + 512]
                rkb_b = rkb.unsqueeze(2).broadcast_to([128, 8, 64])
                P.on('vector', lambda v: v.tensor_tensor(
                    ktm_dst.rearrange("p (h w) -> p h w", w=64),
                    G[bk][:].rearrange("p (h w) -> p h w", w=64), rkb_b, ALU.mult),
                    inc='vp' if not main else None)
                if main:
                    P.on('vector', lambda v: v.tensor_scalar_max(qsd, qsd, 1e-12))
                    P.on('vector', lambda v: v.reciprocal(qrc, qsd))
                    qtm_dst = ab[:, S16_O + j * 1024 + fg * 512:
                                 S16_O + j * 1024 + fg * 512 + 512]
                    qrc_b = qrc.unsqueeze(2).broadcast_to([128, 8, 64])
                    P.on('vector', lambda v: v.tensor_tensor(
                        qtm_dst.rearrange("p (h w) -> p h w", w=64),
                        G[bq][:].rearrange("p (h w) -> p h w", w=64), qrc_b, ALU.mult),
                        inc='vp')
                P.mark(('projepi', seg, j, fg), 'vp')
                P.setguard(f'B{bk}', 'vp')
                if main:
                    P.setguard(f'B{bq}', 'vp')
                P.wmark('scalar', ('projmm', seg, j, fg))
                vtm_dst = ab[:, VTM_O + j * 1024 + fg * 512:
                             VTM_O + j * 1024 + fg * 512 + 512]
                P.on('scalar', lambda s: s.copy(vtm_dst, G[bv][:]), inc='sp')
                P.mark(('vevac', seg, j, fg), 'sp')
                P.setguard(f'B{bv}', 'sp')
        P.mark(('proj-done', seg), 'vp')
        P.mark(('proj-done-s', seg), 'sp')
        P.setguard('pw1', 'tp')
        P.setguard('pw2', 'tp')
        if main:
            P.setguard('pw0', 'tp')

        if os.environ.get("KPH") == "proj":
            continue
        if not main:
            for j in range(CPS):
                g = seg * CPS + j
                P.wmark('tensor', ('projepi', seg, j, 0))
                P.wmark('tensor', ('projepi', seg, j, 1))
                P.wmark('tensor', ('vevac', seg, j, 0))
                P.wmark('tensor', ('vevac', seg, j, 1))
                for h in range(H):
                    kt = ab[:, KTM_O + j * 1024 + h * 64: KTM_O + j * 1024 + h * 64 + 64]
                    vt = ab[:, VTM_O + j * 1024 + h * 64: VTM_O + j * 1024 + h * 64 + 64]
                    sbk = SP[h % 2]
                    off = (h // 2) * 64
                    par = (h % 2) * 64
                    first = (g == 0 and h < 2)
                    P.on('tensor', lambda t, kt=kt, vt=vt, sbk=sbk, off=off, par=par,
                         first=first:
                         t.matmul(sbk[par:par + 64, off:off + 64], kt, vt,
                                  start=first, stop=False, tile_position=(0, par)))
                P.on('tensor', lambda t: t.drain(), inc='tp')
                P.mark(('state', g), 'tp')
            if seg == 1:
                P.wmark('scalar', ('state', 15))
                P.on('scalar', lambda s: s.copy(sm['S'][0:64, :], SP[0][0:64, 0:512]),
                     inc='sp')
                P.on('scalar', lambda s: s.copy(sm['S'][64:128, :], SP[1][64:128, 0:512]),
                     inc='sp')
                P.mark(('sevac', 15), 'sp')
            continue

        P.wmark('tensor', ('proj-done', seg))
        for j in range(CPS):
            for src_o, dst_o, nm in ((KTM_O, KT_O, 'k'), (S16_O, QT_O, 'q')):
                for fg2 in range(2):
                    bank = (j * 2 + fg2) % 2
                    P.wguard('tensor', f'B{bank}')
                    for k in range(4):
                        fc = fg2 * 4 + k
                        src = ab[:, src_o + j * 1024 + fc * 128:
                                 src_o + j * 1024 + fc * 128 + 128]
                        P.on('tensor', lambda t, k=k, bank=bank, src=src:
                             t.transpose(Gb[bank][:, k * 128:(k + 1) * 128], src,
                                         sm['ident'][:]),
                             )
                    P.on('tensor', lambda t: t.drain(), inc='tp')
                    P.mark(('pt', seg, j, nm, fg2), 'tp')
                    P.wmark('scalar', ('pt', seg, j, nm, fg2))
                    dst = packed_dst(dst_o + fg2 * 4 * 1024, j)
                    P.on('scalar', lambda s, bank=bank, dst=dst:
                         s.copy(dst, Gb[bank][:, 0:512].rearrange("p (c r) -> p c r", c=4)),
                         inc='sp')
                    P.mark(('ptc', seg, j, nm, fg2), 'sp')
                    P.setguard(f'B{bank}', 'sp')
        P.mark(('kTqT-done', seg), 'sp')

        P.wguard('gpsimd', 'pw0')
        P.on('gpsimd', lambda g: g.dma_start(
            out=pw(0, 0, 8192).rearrange("p (c f) -> p c f", c=DC),
            in_=A['d_wo'].rearrange("(c p) f -> p c f", p=128)), inc='w', n=16)
        P.mark(('wo', seg), 'w')

        scb = lambda h: (h % 2) + 2 * (h // 8)
        scpos = lambda h: (h % 8) // 2
        P.wmark('tensor', ('kTqT-done', seg))
        P.wmark('vector', 'maskt')
        for j in range(CPS):
            g = 16 + (seg - 2) * CPS + j
            P.wguard('tensor', 'B0', 'B1', 'B2', 'B3')
            for h in range(H):
                par = (h % 2) * 64
                fc = h // 2
                kT = ab[par:par + 64, KT_O + fc * 1024 + j * 128:
                        KT_O + fc * 1024 + j * 128 + 128]
                qT = ab[par:par + 64, QT_O + fc * 1024 + j * 128:
                        QT_O + fc * 1024 + j * 128 + 128]
                bk = scb(h)
                po = scpos(h) * 128
                first = not any(scb(hh) == bk for hh in range(h))
                P.on('tensor', lambda t, kT=kT, qT=qT, bk=bk, po=po, first=first:
                     t.matmul(G[bk][:, po:po + 128], kT, qT, start=first, stop=True))
            P.on('tensor', lambda t: t.drain(), inc='tp')
            P.mark(('sc', g), 'tp')
            P.wguard('tensor', 'SPB')
            for h in range(H):
                kt = ab[:, KTM_O + j * 1024 + h * 64: KTM_O + j * 1024 + h * 64 + 64]
                vt = ab[:, VTM_O + j * 1024 + h * 64: VTM_O + j * 1024 + h * 64 + 64]
                sbk = SP[h % 2]
                off = (h // 2) * 64
                par = (h % 2) * 64
                P.on('tensor', lambda t, kt=kt, vt=vt, sbk=sbk, off=off, par=par:
                     t.matmul(sbk[par:par + 64, off:off + 64], kt, vt,
                              start=False, stop=False, tile_position=(0, par)))
            P.on('tensor', lambda t: t.drain(), inc='tp')
            P.mark(('state', g), 'tp')
            P.wmark('vector', ('sc', g))
            for bk in range(4):
                P.on('vector', lambda v, bk=bk:
                     v.tensor_tensor(ab[:, MSC_O + bk * 512: MSC_O + (bk + 1) * 512],
                                     G[bk][:], sm['maskt'][:], ALU.mult), inc='vp')
                P.setguard(f'B{bk}', 'vp')
            P.mark(('msc', g), 'vp')
            P.wmark('tensor', ('msc', g))
            P.wmark('tensor', ('sevac', g - 1))
            for hh in range(2):
                P.wguard('tensor', 'B4', 'B5')
                for h in range(hh * 8, hh * 8 + 8):
                    ob = 4 + (h % 2)
                    par = (h % 2) * 64
                    po = ((h % 8) // 2) * 128
                    vt = ab[:, VTM_O + j * 1024 + h * 64: VTM_O + j * 1024 + h * 64 + 64]
                    ms = ab[:, MSC_O + scb(h) * 512 + scpos(h) * 128:
                            MSC_O + scb(h) * 512 + scpos(h) * 128 + 128]
                    first = (h - hh * 8) < 2
                    P.on('tensor', lambda t, vt=vt, ms=ms, ob=ob, par=par, po=po,
                         first=first:
                         t.matmul(G[ob][par:par + 64, po:po + 128], vt, ms,
                                  start=first, stop=False, tile_position=(0, par)))
                    Ssl = sm['S'][par:par + 64, (h // 2) * 64: (h // 2) * 64 + 64]
                    fc = h // 2
                    qT = ab[par:par + 64, QT_O + fc * 1024 + j * 128:
                            QT_O + fc * 1024 + j * 128 + 128]
                    P.on('tensor', lambda t, Ssl=Ssl, qT=qT, ob=ob, par=par, po=po:
                         t.matmul(G[ob][par:par + 64, po:po + 128], Ssl, qT,
                                  start=False, stop=True, tile_position=(par, par)))
                P.on('tensor', lambda t: t.drain(), inc='tp')
                P.mark(('omm', g, hh), 'tp')
                P.wmark('scalar', ('omm', g, hh))
                src_e = G[4][0:64, 0:512].rearrange("p (c r) -> p c r", c=4)
                dst_e = packed_dst(PW_O + 8192 + hh * 4 * 1024, j)[0:64]
                P.on('scalar', lambda s, src_e=src_e, dst_e=dst_e:
                     s.copy(dst_e, src_e), inc='sp')
                src_o2 = G[5][64:128, 0:512].rearrange("p (c r) -> p c r", c=4)
                dst_o2 = packed_dst(PW_O + 8192 + hh * 4 * 1024, j)[64:128]
                P.on('scalar', lambda s, src_o2=src_o2, dst_o2=dst_o2:
                     s.copy(dst_o2, src_o2), inc='sp')
                P.setguard('B4', 'sp')
                P.setguard('B5', 'sp')
            P.mark(('omm', g), 'tp')
            P.mark(('oevac', g), 'sp')
            P.wmark('scalar', ('omm', g))
            P.on('scalar', lambda s: s.copy(sm['S'][0:64, :], SP[0][0:64, 0:512]),
                 inc='sp')
            P.on('scalar', lambda s: s.copy(sm['S'][64:128, :], SP[1][64:128, 0:512]),
                 inc='sp')
            P.mark(('sevac', g), 'sp')
            P.setguard('SPB', 'sp')
        P.mark(('attn-done', seg), 'sp')

        P.wmark('tensor', ('attn-done', seg))
        P.wmark('tensor', ('wo', seg))
        for j in range(CPS):
            for dg in range(2):
                yb = (j * 2 + dg) % 2
                P.wguard('tensor', f'B{yb}')
                for fc in range(DC):
                    aoT = ab[:, PW_O + 8192 + fc * 1024 + j * 128:
                             PW_O + 8192 + fc * 1024 + j * 128 + 128]
                    P.on('tensor', lambda t, aoT=aoT, fc=fc, dg=dg, yb=yb:
                         t.matmul(G[yb][:], aoT, pw(0, fc * 1024 + dg * 512, 512),
                                  start=(fc == 0), stop=(fc == DC - 1)))
                P.on('tensor', lambda t: t.drain(), inc='tp')
                P.mark(('wo-mm', seg, j, dg), 'tp')
                P.wmark('scalar', ('wo-mm', seg, j, dg))
                P.wguard('scalar', f'stg{j % 2}')
                stg = ab[:, STG_O + (j % 2) * 1024 + dg * 512:
                         STG_O + (j % 2) * 1024 + dg * 512 + 512]
                P.on('scalar', lambda s, stg=stg, yb=yb: s.copy(stg, G[yb][:]),
                     inc='sp')
                P.setguard(f'B{yb}', 'sp')
            P.mark(('s-stage', seg, j), 'sp')
            P.wmark('sync', ('s-stage', seg, j))
            row = (seg - 2) * 1024 + j * 128
            P.on('sync', lambda s, row=row, j=j: s.dma_start(
                out=A['d_sscr'][row:row + 128, :],
                in_=ab[:, STG_O + (j % 2) * 1024: STG_O + (j % 2) * 1024 + 1024]),
                inc='o', n=16)
            P.mark(('s-store', seg, j), 'o')
            P.setguard(f'stg{j % 2}', 'o')
        P.mark(('wo-done', seg), 'tp')
        P.setguard('pw0', 'tp')
        P.setguard('pw1', 'tp')

        if KDBG2 and seg == 3:
            P.wmark('sync', ('wo-done', seg))
            for nm3, off3 in (('dkT', KT_O), ('dqT', QT_O), ('dktm', KTM_O),
                              ('dvtm', VTM_O)):
                for ci in range(8):
                    P.on('sync', lambda s, nm3=nm3, off3=off3, ci=ci: s.dma_start(
                        out=A['d_' + nm3][:, ci * 1024:(ci + 1) * 1024],
                        in_=ab[:, off3 + ci * 1024: off3 + (ci + 1) * 1024]),
                        inc='o', n=16)
            P.on('sync', lambda s: s.dma_start(out=A['d_dS2'], in_=sm['S'][:]),
                 inc='o', n=16)
            P.mark(('kdbg2', seg), 'o')
        for rname in ('w1s0', 'w1s1', 'w3s0', 'w3s1', 'w2h0', 'w2h1'):
            P.guard[rname] = P.marks[('wo-done', seg)]
            if KDBG2 and seg == 3:
                P.guard[rname] = P.marks[('kdbg2', seg)]
        if KDBG2 and seg == 3:
            P.wmark('vector', ('kdbg2', seg))
            P.wmark('gpsimd', ('kdbg2', seg))
        w1s = [ab[:, W13_O + i * 4096: W13_O + (i + 1) * 4096] for i in range(2)]
        w3s = [ab[:, W13_O + 8192 + i * 4096: W13_O + 8192 + (i + 1) * 4096]
               for i in range(2)]
        w2h = [ab[:, W2_O + i * 11264: W2_O + (i + 1) * 11264] for i in range(2)]

        def w13_dma(e2, gi):
            hf0, ntile = GRP[gi]
            sl = gi % 2
            cols = ntile * 128
            for (mat, dram, slots) in (('w1', A['d_w1t'], w1s),
                                       ('w3', A['d_w3t'], w3s)):
                P.wguard('gpsimd', f'{mat}s{sl}')
                dst = slots[sl].rearrange("p (c f) -> p c f", f=512)[:, :, 0:cols]
                src = dram[e2, :, hf0 * 128: hf0 * 128 + cols].rearrange(
                    "(c p) f -> p c f", p=128)
                P.on('gpsimd', lambda g2, dst=dst, src=src:
                     g2.dma_start(out=dst, in_=src), inc='w', n=16)
                P.mark(('wldr', seg, e2, gi, mat), 'w')

        for e in range(E):
            w13_dma(e, 0)
            w13_dma(e, 1)
            for dg in range(2):
                P.wguard('gpsimd', f'w2h{dg}')
                dst = w2h[dg].rearrange("p (c f) -> p c f", f=512)
                src = A['d_w2t'][e, :, dg * 512:(dg + 1) * 512].rearrange(
                    "(c p) f -> p c f", p=128)
                P.on('gpsimd', lambda g2, dst=dst, src=src:
                     g2.dma_start(out=dst, in_=src), inc='w', n=16)
                P.mark(('w2ldr', seg, e, dg), 'w')

            if e == 0:
                P.wmark('vector', ('s-store', seg, CPS - 1))
            if KDBG3 and seg == 2 and e == 1:
                P.wmark('sync', ('htt', seg, 0, 21))
                for ci in range(22):
                    P.on('sync', lambda s, ci=ci: s.dma_start(
                        out=A['d_dh'][:, ci * 1024:(ci + 1) * 1024],
                        in_=ab[:, H_O + ci * 1024: H_O + (ci + 1) * 1024]),
                        inc='o', n=16)
                P.mark(('kdbg3',), 'o')
                P.wmark('scalar', ('kdbg3',))
                P.wmark('vector', ('kdbg3',))
                P.wmark('tensor', ('kdbg3',))
            for gi, (hf0, ntile) in enumerate(GRP):
                sl = gi % 2
                P.wmark('tensor', ('wldr', seg, e, gi, 'w1'))
                P.wmark('tensor', ('wldr', seg, e, gi, 'w3'))
                for k in range(ntile):
                    hf = hf0 + k
                    P.wguard('tensor', 'B0', 'B1')
                    for c in range(DC):
                        lt = w1s[sl][:, c * 512 + k * 128: c * 512 + k * 128 + 128]
                        for sub in range(2):
                            P.on('tensor', lambda t, lt=lt, c=c, sub=sub:
                                 t.matmul(G[sub][:], lt,
                                          ab[:, XNT_O + c * 1024 + sub * 512:
                                             XNT_O + c * 1024 + sub * 512 + 512],
                                          start=(c == 0), stop=(c == DC - 1)))
                    P.on('tensor', lambda t: t.drain(), inc='tp')
                    P.mark(('a-mm', seg, e, hf), 'tp')
                    P.wguard('tensor', 'B2', 'B3')
                    for c in range(DC):
                        lt = w3s[sl][:, c * 512 + k * 128: c * 512 + k * 128 + 128]
                        for sub in range(2):
                            P.on('tensor', lambda t, lt=lt, c=c, sub=sub:
                                 t.matmul(G[2 + sub][:], lt,
                                          ab[:, XNT_O + c * 1024 + sub * 512:
                                             XNT_O + c * 1024 + sub * 512 + 512],
                                          start=(c == 0), stop=(c == DC - 1)))
                    P.on('tensor', lambda t: t.drain(), inc='tp')
                    P.mark(('b-mm', seg, e, hf), 'tp')
                    P.wmark('scalar', ('a-mm', seg, e, hf))
                    P.wguard('scalar', 'asil')
                    for sub in range(2):
                        P.on('scalar', lambda s, sub=sub:
                             s.activation(af[:, ASF_O + sub * 512: ASF_O + sub * 512 + 512],
                                          G[sub][:], AF.Silu),
                             inc='sp')
                        P.setguard(f'B{sub}', 'sp')
                    P.mark(('silu', seg, e, hf), 'sp')
                    P.wmark('vector', ('silu', seg, e, hf))
                    P.wmark('vector', ('b-mm', seg, e, hf))
                    for sub in range(2):
                        P.on('vector', lambda v, sub=sub, hf=hf:
                             v.tensor_tensor(ab[:, H_O + hf * 1024 + sub * 512:
                                                H_O + hf * 1024 + sub * 512 + 512],
                                             af[:, ASF_O + sub * 512:
                                                ASF_O + sub * 512 + 512],
                                             G[2 + sub][:], ALU.mult),
                             inc='vp' if sub == 1 else None)
                        P.setguard(f'B{2 + sub}', 'vp')
                    P.mark(('htt', seg, e, hf), 'vp')
                    P.setguard('asil', 'vp')
                P.setguard(f'w1s{sl}', 'tp')
                P.setguard(f'w3s{sl}', 'tp')
                if gi + 2 < len(GRP):
                    w13_dma(e, gi + 2)

            P.wmark('tensor', ('htt', seg, e, 21))
            P.wmark('tensor', ('w2ldr', seg, e, 0))
            P.wmark('tensor', ('w2ldr', seg, e, 1))
            for jj in range(CPS):
                for dg in range(2):
                    ti = jj * 2 + dg
                    yb = 4 + ti % 2
                    P.wguard('tensor', f'B{yb}')
                    for hf in range(NHFT):
                        hp = ab[:, H_O + hf * 1024 + jj * 128:
                                H_O + hf * 1024 + jj * 128 + 128]
                        P.on('tensor', lambda t, hp=hp, hf=hf, dg=dg, yb=yb:
                             t.matmul(G[yb][:], hp, w2h[dg][:, hf * 512:(hf + 1) * 512],
                                      start=(hf == 0), stop=(hf == NHFT - 1)))
                    P.on('tensor', lambda t: t.drain(), inc='tp')
                    P.mark(('y-mm', seg, e, ti), 'tp')
                    P.wmark('vector', ('y-mm', seg, e, ti))
                    ysl = af[:, Y_O + jj * 1024 + dg * 512: Y_O + jj * 1024 + dg * 512 + 512]
                    cwsl = sm['cw'][:, jj * 8 + e: jj * 8 + e + 1]
                    if e == 0:
                        P.on('vector', lambda v, ysl=ysl, yb=yb, cwsl=cwsl:
                             v.tensor_scalar_mul(ysl, G[yb][:], cwsl), inc='vp')
                    else:
                        P.on('vector', lambda v, ysl=ysl, yb=yb, cwsl=cwsl:
                             v.scalar_tensor_tensor(ysl, G[yb][:], cwsl, ysl,
                                                    ALU.mult, ALU.add), inc='vp')
                    P.setguard(f'B{yb}', 'vp')
                    P.mark(('stt', seg, e, ti), 'vp')
            P.setguard('w2h0', 'tp')
            P.setguard('w2h1', 'tp')
        P.mark(('w2-done', seg), 'tp')
        P.setguard('pw0', 'tp')
        P.setguard('pw1', 'tp')
        P.setguard('pw2', 'tp')

        for jj in range(CPS):
            row = (seg - 2) * 1024 + jj * 128
            P.wmark('gpsimd', ('s-store', seg, jj))
            P.wguard('gpsimd', f'sld{jj % 2}')
            sld = af[:, SLD_O + (jj % 2) * 1024: SLD_O + (jj % 2) * 1024 + 1024]
            P.on('gpsimd', lambda g2, sld=sld, row=row: g2.dma_start(
                out=sld, in_=A['d_sscr'][row:row + 128, :]), inc='w', n=16)
            P.mark(('s-ld', seg, jj), 'w')
            xr = af[:, XR_O + (jj % 2) * 1024: XR_O + (jj % 2) * 1024 + 1024]
            P.wguard('sync', f'xr{jj % 2}')
            P.on('sync', lambda s, xr=xr, row=row: s.dma_start(
                out=xr, in_=A['d_xseq'][2048 + row: 2048 + row + 128, :]), inc='x', n=16)
            P.mark(('comb-x', seg, jj), 'x')
            for dg in range(2):
                ysl = af[:, Y_O + jj * 1024 + dg * 512: Y_O + jj * 1024 + dg * 512 + 512]
                P.wmark('scalar', ('stt', seg, E - 1, jj * 2 + dg))
                P.wguard('scalar', 'gs')
                gsc = af[:, GS_O + dg * 512: GS_O + dg * 512 + 512]
                P.on('scalar', lambda s, gsc=gsc, ysl=ysl:
                     s.activation(gsc, ysl, AF.Silu), inc='sp')
                P.mark(('csilu', seg, jj, dg), 'sp')
                P.wmark('vector', ('csilu', seg, jj, dg))
                P.wmark('vector', ('s-ld', seg, jj))
                P.wmark('vector', ('comb-x', seg, jj))
                sgs = af[:, SLD_O + (jj % 2) * 1024 + dg * 512:
                         SLD_O + (jj % 2) * 1024 + dg * 512 + 512]
                xsl = af[:, XR_O + (jj % 2) * 1024 + dg * 512:
                         XR_O + (jj % 2) * 1024 + dg * 512 + 512]
                P.on('vector', lambda v, gsc=gsc, sgs=sgs: v.tensor_mul(gsc, gsc, sgs))
                P.on('vector', lambda v, xsl=xsl, gsc=gsc: v.tensor_add(xsl, gsc, xsl),
                     inc='vp')
            P.setguard('gs', 'vp')
            P.setguard(f'sld{jj % 2}', 'vp')
            P.mark(('comb-v', seg, jj), 'vp')
            P.wmark('sync', ('comb-v', seg, jj))
            P.on('sync', lambda s, xr=xr, row=row: s.dma_start(
                out=A['d_out'][row:row + 128, :], in_=xr), inc='o', n=16)
            P.mark(('out-store', seg, jj), 'o')
            P.setguard(f'xr{jj % 2}', 'o')

    if KDBG:
        P.w('sync', 'tp', P.cnt['tp'])
        P.w('sync', 'vp', P.cnt['vp'])
        P.w('sync', 'sp', P.cnt['sp'])
        P.w('sync', 'w', P.cnt['w'])
        for ci in range(32):
            P.on('sync', lambda s, ci=ci: s.dma_start(
                out=A['d_dab'][:, ci * 1024:(ci + 1) * 1024],
                in_=ab[:, ci * 1024:(ci + 1) * 1024]), inc='o', n=16)
        for ci in range(8):
            P.on('sync', lambda s, ci=ci: s.dma_start(
                out=A['d_daf'][:, ci * 1024:(ci + 1) * 1024],
                in_=af[:, ci * 1024:(ci + 1) * 1024]), inc='o', n=16)
        P.on('sync', lambda s: s.dma_start(out=A['d_dsm'], in_=sm['S'][:]), inc='o', n=16)
        P.on('sync', lambda s: s.dma_start(out=A['d_dcw'], in_=sm['cw'][:]), inc='o', n=16)
        P.on('sync', lambda s: s.dma_start(out=A['d_dbs'], in_=sm['bseg'][:]), inc='o', n=16)
        P.on('sync', lambda s: s.dma_start(out=A['d_dqn'], in_=sm['qn'][:]), inc='o', n=16)
    P.w('sync', 'o', P.cnt['o'])


def build_nc():
    nc = bass.Bass("TRN2", target_bir_lowering=False)
    A = {}
    A['d_xseq'] = nc.dram_tensor("xseq", [VIEW, D], F32, kind="ExternalInput").ap()
    for nmW in ('wq', 'wk', 'wv', 'wo'):
        A['d_' + nmW] = nc.dram_tensor(nmW, [D, D], F32, kind="ExternalInput").ap()
    A['d_wb'] = nc.dram_tensor("wb", [D, H], F32, kind="ExternalInput").ap()
    A['d_wgt'] = nc.dram_tensor("wgt", [D, E], F32, kind="ExternalInput").ap()
    A['d_bbb'] = nc.dram_tensor("bbb", [128, H], F32, kind="ExternalInput").ap()
    A['d_w1t'] = nc.dram_tensor("w1t", [E, D, HF], F32, kind="ExternalInput").ap()
    A['d_w3t'] = nc.dram_tensor("w3t", [E, D, HF], F32, kind="ExternalInput").ap()
    A['d_w2t'] = nc.dram_tensor("w2t", [E, HF, D], F32, kind="ExternalInput").ap()
    A['d_maskt'] = nc.dram_tensor("maskt", [128, 512], F32, kind="ExternalInput").ap()
    A['d_ident'] = nc.dram_tensor("identin", [128, 128], F32, kind="ExternalInput").ap()
    A['d_out'] = nc.dram_tensor("out", [2048, D], F32, kind="ExternalOutput").ap()
    A['d_sscr'] = nc.dram_tensor(
        "sscr", [2048, D], BF16,
        kind="ExternalOutput" if (KDBG or KDBG2) else "Internal").ap()
    if KDBG:
        A['d_dab'] = nc.dram_tensor("dab", [128, 32768], BF16, kind="ExternalOutput").ap()
        A['d_daf'] = nc.dram_tensor("daf", [128, 8192], F32, kind="ExternalOutput").ap()
        A['d_dsm'] = nc.dram_tensor("dsm", [128, 512], BF16, kind="ExternalOutput").ap()
        A['d_dcw'] = nc.dram_tensor("dcw", [128, 64], F32, kind="ExternalOutput").ap()
        A['d_dbs'] = nc.dram_tensor("dbs", [128, 128], F32, kind="ExternalOutput").ap()
        A['d_dqn'] = nc.dram_tensor("dqn", [128, 112], F32, kind="ExternalOutput").ap()
    if KDBG3:
        A['d_dh'] = nc.dram_tensor("dh", [128, 22528], BF16, kind="ExternalOutput").ap()
    if KDBG2:
        for nm3 in ('dkT', 'dqT', 'dktm', 'dvtm'):
            A['d_' + nm3] = nc.dram_tensor(nm3, [128, 8192], BF16, kind="ExternalOutput").ap()
        A['d_dS2'] = nc.dram_tensor("dS2", [128, 512], BF16, kind="ExternalOutput").ap()

    ctx = ExitStack()
    ab = ctx.enter_context(nc.sbuf_tensor("ab", [128, AB_COLS], BF16))
    af = ctx.enter_context(nc.sbuf_tensor("af", [128, AF_COLS], F32))
    smh = {}
    for nm2, shape, dt in (('S', [128, 512], BF16), ('maskt', [128, 512], F32),
                           ('ident', [128, 128], BF16), ('bbb', [128, 16], F32),
                           ('wb', [128, 128], BF16), ('wg', [128, 64], BF16),
                           ('stats', [128, 32], F32), ('btmp', [128, 128], F32),
                           ('bseg', [128, 128], F32), ('rtr', [128, 8], F32),
                           ('ex', [128, 48], F32), ('cw', [128, 64], F32),
                           ('qn', [128, 112], F32)):
        smh[nm2] = ctx.enter_context(nc.sbuf_tensor(nm2 + "_sb", shape, dt))
    SPt = [ctx.enter_context(nc.psum_tensor(f"SP{i}", [128, 512], F32)) for i in range(2)]
    Gt = [ctx.enter_context(nc.psum_tensor(f"G{i}", [128, 512], F32)) for i in range(6)]
    sems = {s: ctx.enter_context(nc.semaphore(s)) for s in
            ('w', 'x', 'o', 'tp', 'vp', 'sp')}

    A['ab'], A['af'] = ab[:], af[:]
    A['sm'] = {k: v[:] for k, v in smh.items()}
    A['G'] = [g[:] for g in Gt]
    A['SP'] = [s[:] for s in SPt]
    A['Gb'] = [A['G'][i].bitcast(BF16) for i in range(2)]

    shared_marks = {}
    passes = {}
    with nc.Block() as blk:
        for which, deco in (('tensor', blk.tensor), ('vector', blk.vector),
                            ('scalar', blk.scalar), ('gpsimd', blk.gpsimd),
                            ('sync', blk.sync)):
            Pp = Pass(nc, which, sems, shared_marks)
            passes[which] = Pp
            deco(lambda eng, Pp=Pp: emit(Pp, A))
    ref = passes['tensor'].cnt
    for w2_, p2 in passes.items():
        assert p2.cnt == ref, (w2_, p2.cnt, ref)
    ctx.close()
    return nc


def kernel(L0_attn_x, L0_attn_Wq, L0_attn_Wk, L0_attn_Wv, L0_attn_Wo,
           L0_attn_Wb, L0_attn_bb, L0_ffn_Wg, L0_ffn_W1, L0_ffn_W2, L0_ffn_W3):
    x = np.ascontiguousarray(np.asarray(L0_attn_x, dtype=np.float32))
    wq = np.ascontiguousarray(np.asarray(L0_attn_Wq, dtype=np.float32))
    wk = np.ascontiguousarray(np.asarray(L0_attn_Wk, dtype=np.float32))
    wv = np.ascontiguousarray(np.asarray(L0_attn_Wv, dtype=np.float32))
    wo = np.ascontiguousarray(np.asarray(L0_attn_Wo, dtype=np.float32))
    wb = np.ascontiguousarray(np.asarray(L0_attn_Wb, dtype=np.float32))
    bb = np.asarray(L0_attn_bb, dtype=np.float32)
    wgt = np.ascontiguousarray(np.asarray(L0_ffn_Wg, dtype=np.float32).T)
    w1t = np.ascontiguousarray(np.asarray(L0_ffn_W1, dtype=np.float32).transpose(0, 2, 1))
    w3t = np.ascontiguousarray(np.asarray(L0_ffn_W3, dtype=np.float32).transpose(0, 2, 1))
    w2t = np.ascontiguousarray(np.asarray(L0_ffn_W2, dtype=np.float32).transpose(0, 2, 1))
    bbb = np.ascontiguousarray(np.broadcast_to(bb[None, :], (128, H)))
    maskt = np.ascontiguousarray(
        np.tile(np.triu(np.ones((128, 128), dtype=np.float32)), (1, 4)))
    ident = np.eye(128, dtype=np.float32)

    zeros = np.zeros((2048, D), dtype=np.float32)
    in_maps = []
    for core in range(8):
        bidx, half = core // 2, core % 2
        if half == 0:
            xseq = np.ascontiguousarray(
                np.concatenate([zeros, x[bidx, 0:2048]], axis=0))
        else:
            xseq = np.ascontiguousarray(x[bidx])
        in_maps.append({
            "xseq": xseq, "wq": wq, "wk": wk, "wv": wv, "wo": wo, "wb": wb,
            "wgt": wgt, "bbb": bbb, "w1t": w1t, "w3t": w3t, "w2t": w2t,
            "maskt": maskt, "identin": ident,
        })

    nc = build_nc()
    res = run_bass_kernel_spmd(nc, in_maps, core_ids=list(range(8)))
    out = np.empty((B, L, D), dtype=np.float32)
    for core in range(8):
        bidx, half = core // 2, core % 2
        out[bidx, half * 2048:(half + 1) * 2048] = res.results[core]["out"]
    if KDBG or KDBG2 or KDBG3:
        import pickle
        with open('/tmp/kdbg.pkl', 'wb') as fh:
            pickle.dump([{k: np.asarray(v) for k, v in r.items()}
                         for r in res.results], fh)
    return out


# version 10
# speedup vs baseline: 1.0610x; 1.0610x over previous
import os
import numpy as np
from contextlib import ExitStack

KDBG = os.environ.get("KDBG") == "1"
KDBG2 = os.environ.get("KDBG2") == "1"
KDBG3 = os.environ.get("KDBG3") == "1"


import concourse.bass as bass
import concourse.mybir as mybir
from concourse.bass_utils import run_bass_kernel_spmd

F32 = mybir.dt.float32
BF16 = mybir.dt.bfloat16
AF = mybir.ActivationFunctionType
ALU = mybir.AluOpType
AX = mybir.AxisListType

B, L, D = 4, 4096, 1024
H, DH = 16, 64
E, HF = 8, 2816
NHFT = 22
VIEW, SEG, NSEG, CPS, DC = 4096, 1024, 4, 8, 8

XNT_O = 0
S16_O = 8192
KTM_O = 16384
VTM_O = 24576
KT_O = 32768
QT_O = 40960
PW_O = 49152
AB_COLS = 73728
H_O = 8192
W13_O = 32768
W2_O = 49152
MSC_O = S16_O
STG_O = S16_O + 2048

Y_O = 0
XR_O = 8192
GS_O = 10240
SLD_O = 11264
ASF_O = 13312
AF_COLS = 14336

GRP = [(0, 4), (4, 4), (8, 4), (12, 4), (16, 4), (20, 2)]


class Pass:
    def __init__(self, nc, which, sems, shared_marks):
        self.nc = nc
        self.which = which
        self.sems = sems
        self.cnt = {k: 0 for k in sems}
        self.marks = shared_marks
        self.lastw = {}
        self.guard = {}

    def on(self, eng, fn, inc=None, n=1):
        if eng == 'vector':
            self.w('vector', 'vp', self.cnt['vp'])
            if inc is None:
                inc, n = 'vp', 1
        if self.which == eng:
            inst = fn(getattr(self.nc, eng))
            if inc is not None:
                inst.then_inc(self.sems[inc], n)
        if inc is not None:
            self.cnt[inc] += n

    def w(self, eng, sem, val):
        if val <= 0:
            return
        key = (eng, sem)
        if self.lastw.get(key, -1) >= val:
            return
        self.lastw[key] = val
        if self.which == eng:
            getattr(self.nc, eng).wait_ge(self.sems[sem], val)

    def mark(self, key, sem):
        v = (sem, self.cnt[sem])
        if key in self.marks:
            assert self.marks[key] == v, (key, self.marks[key], v, self.which)
        else:
            self.marks[key] = v
        return v

    def wmark(self, eng, key):
        sem, v = self.marks[key]
        self.w(eng, sem, v)

    def wguard(self, eng, *res):
        for r in res:
            g = self.guard.get(r)
            if g is not None:
                self.w(eng, g[0], g[1])

    def setguard(self, res, sem):
        self.guard[res] = (sem, self.cnt[sem])


def emit(P, A):
    ab, af = A["ab"], A["af"]
    G, SP, Gb, sm = A["G"], A["SP"], A["Gb"], A["sm"]

    def pw(i, off, n):
        return ab[:, PW_O + i * 8192 + off: PW_O + i * 8192 + off + n]

    def packed_dst(base, j, r=1024):
        st = base + j * 128
        return ab[:, st: st + 4 * r].rearrange("p (c r) -> p c r", r=r)[:, :, 0:128]

    P.on('gpsimd', lambda g: g.dma_start(out=sm['ident'][:], in_=A['d_ident']),
         inc='wc', n=16)
    P.on('gpsimd', lambda g: g.dma_start(
        out=sm['wb'][:].rearrange("p (c f) -> p c f", c=DC),
        in_=A['d_wb'].rearrange("(c p) f -> p c f", p=128)), inc='wc', n=16)
    P.on('gpsimd', lambda g: g.dma_start(
        out=sm['wg'][:].rearrange("p (c f) -> p c f", c=DC),
        in_=A['d_wgt'].rearrange("(c p) f -> p c f", p=128)), inc='wc', n=16)
    for k2 in ('ident', 'wb', 'wg'):
        P.mark(k2, 'wc')
    P.on('sync', lambda s: s.dma_start(out=sm['maskt'][:], in_=A['d_maskt']),
         inc='xc', n=16)
    P.on('sync', lambda s: s.dma_start(out=sm['bbb'][:], in_=A['d_bbb']),
         inc='xc', n=16)
    for k2 in ('maskt', 'bbb'):
        P.mark(k2, 'xc')

    nseg_emit = int(os.environ.get("KSEGS", str(NSEG)))
    for seg in range(nseg_emit):
        main = seg >= 2
        tok0 = seg * SEG

        if seg in (0, 3):
            P.wguard('gpsimd', 'pw1')
            P.on('gpsimd', lambda g: g.dma_start(
                out=pw(1, 0, 8192).rearrange("p (c f) -> p c f", c=DC),
                in_=A['d_wk'].rearrange("(c p) f -> p c f", p=128)), inc='pw1', n=16)
            P.mark(('wk', seg), 'pw1')
            P.wguard('gpsimd', 'pw2')
            P.on('gpsimd', lambda g: g.dma_start(
                out=pw(2, 0, 8192).rearrange("p (c f) -> p c f", c=DC),
                in_=A['d_wv'].rearrange("(c p) f -> p c f", p=128)), inc='pw2', n=16)
            P.mark(('wv', seg), 'pw2')
        if main:
            P.wguard('gpsimd', 'pw0')
            P.on('gpsimd', lambda g: g.dma_start(
                out=pw(0, 0, 8192).rearrange("p (c f) -> p c f", c=DC),
                in_=A['d_wq'].rearrange("(c p) f -> p c f", p=128)), inc='pw0', n=16)
            P.mark(('wq', seg), 'pw0')

        for j in range(CPS):
            xr = af[:, XR_O + (j % 2) * 1024: XR_O + (j % 2) * 1024 + 1024]
            P.wguard('sync', f'xr{j % 2}')
            P.on('sync', lambda s: s.dma_start(
                out=xr, in_=A['d_xseq'][tok0 + j * 128: tok0 + (j + 1) * 128, :]),
                inc=f'xr{j % 2}', n=16)
            P.mark(('lnx', seg, j), f'xr{j % 2}')

            st = sm['stats']
            c0 = (j % 2) * 16
            msum, mneg = st[:, c0:c0 + 1], st[:, c0 + 2:c0 + 3]
            ssum, rstd = st[:, c0 + 1:c0 + 2], st[:, c0 + 3:c0 + 4]
            ex2, m2 = st[:, c0 + 4:c0 + 5], st[:, c0 + 5:c0 + 6]
            var, sd = st[:, c0 + 6:c0 + 7], st[:, c0 + 7:c0 + 8]
            P.wmark('vector', ('lnx', seg, j))
            P.wmark('scalar', ('lnx', seg, j))
            P.on('vector', lambda v: v.reduce_sum(msum, xr, axis=AX.X))
            P.on('vector', lambda v: v.tensor_scalar_mul(mneg, msum, -1.0 / 1024.0))
            sqd = af[:, GS_O:GS_O + 1024]
            P.wguard('scalar', 'gs')
            P.on('scalar', lambda s: s.activation(sqd, xr, AF.Square, accum_out=ssum),
                 inc='sp')
            P.mark(('lnsq', seg, j), 'sp')
            P.wmark('vector', ('lnsq', seg, j))
            P.on('vector', lambda v: v.tensor_scalar_mul(ex2, ssum, 1.0 / 1024.0))
            P.on('vector', lambda v: v.tensor_mul(m2, mneg, mneg))
            P.on('vector', lambda v: v.tensor_sub(var, ex2, m2))
            P.on('vector', lambda v: v.tensor_scalar_add(var, var, 1e-5), inc='vp')
            P.mark(('lnvar', seg, j), 'vp')
            P.wmark('scalar', ('lnvar', seg, j))
            P.on('scalar', lambda s: s.activation(sd, var, AF.Sqrt),
                 inc='sp')
            P.mark(('lnsd', seg, j), 'sp')
            P.wmark('vector', ('lnsd', seg, j))
            P.on('vector', lambda v: v.reciprocal(rstd, sd))
            xn = ab[:, S16_O + j * 1024: S16_O + (j + 1) * 1024]
            P.on('vector', lambda v: v.tensor_scalar(xn, xr, mneg, rstd,
                                                     ALU.add, ALU.mult), inc='vp')
            P.mark(('lnap', seg, j), 'vp')
            P.setguard(f'xr{j % 2}', 'vp')

        P.wmark('tensor', 'ident')
        for j in range(CPS):
            P.wmark('tensor', ('lnap', seg, j))
            for fg2 in range(2):
                bank = (j * 2 + fg2) % 2
                P.wguard('tensor', f'B{bank}')
                for k in range(4):
                    fc = fg2 * 4 + k
                    src = ab[:, S16_O + j * 1024 + fc * 128:
                             S16_O + j * 1024 + fc * 128 + 128]
                    P.on('tensor', lambda t, k=k, bank=bank, src=src:
                         t.transpose(Gb[bank][:, k * 128:(k + 1) * 128], src,
                                     sm['ident'][:]),
                         )
                P.on('tensor', lambda t: t.drain(), inc='tp')
                P.mark(('xt', seg, j, fg2), 'tp')
                P.wmark('scalar', ('xt', seg, j, fg2))
                dst = packed_dst(XNT_O + fg2 * 4 * 1024, j)
                P.on('scalar', lambda s, bank=bank, dst=dst:
                     s.copy(dst, Gb[bank][:, 0:512].rearrange("p (c r) -> p c r", c=4)),
                     inc='sp')
                P.mark(('xnT', seg, j, fg2), 'sp')
                P.setguard(f'B{bank}', 'sp')
        P.mark(('xnT-done', seg), 'sp')

        if os.environ.get("KPH") == "ln":
            continue
        P.wmark('tensor', 'wb')
        if main:
            P.wmark('tensor', 'wg')
        P.wguard('tensor', 'B2')
        for j in range(CPS):
            P.wmark('tensor', ('xnT', seg, j, 0))
            P.wmark('tensor', ('xnT', seg, j, 1))
            for c in range(DC):
                xt = ab[:, XNT_O + c * 1024 + j * 128: XNT_O + c * 1024 + j * 128 + 128]
                first = (j == 0 and c == 0)
                last = (c == DC - 1)
                if main:
                    P.on('tensor', lambda t, xt=xt, j=j, first=first, last=last:
                         t.matmul(G[2][:, j * 24: j * 24 + 16], xt,
                                  sm['wb'][:, c * 16:(c + 1) * 16],
                                  start=first, stop=last))
                    P.on('tensor', lambda t, xt=xt, j=j, last=last:
                         t.matmul(G[2][:, j * 24 + 16: j * 24 + 24], xt,
                                  sm['wg'][:, c * 8:(c + 1) * 8],
                                  start=False, stop=last))
                else:
                    P.on('tensor', lambda t, xt=xt, j=j, first=first, last=last:
                         t.matmul(G[2][:, j * 24: j * 24 + 16], xt,
                                  sm['wb'][:, c * 16:(c + 1) * 16],
                                  start=first, stop=last))
            P.on('tensor', lambda t: t.drain(), inc='tp')
            P.mark(('pre', seg, j), 'tp')
        P.wmark('vector', ('pre', seg, CPS - 1))
        P.wmark('vector', 'bbb')
        bsl = G[2][:, 0:192].rearrange("p (j f) -> p j f", j=8)[:, :, 0:16]
        bbb_b = sm['bbb'][:].unsqueeze(1).broadcast_to([128, 8, 16])
        P.on('vector', lambda v: v.tensor_tensor(
            sm['btmp'][:].rearrange("p (j f) -> p j f", j=8), bsl, bbb_b, ALU.add),
            inc='vp')
        P.mark(('btmp', seg), 'vp')
        P.wmark('scalar', ('btmp', seg))
        P.on('scalar', lambda s: s.activation(sm['bseg'][:], sm['btmp'][:], AF.Sigmoid),
             inc='sp')
        P.mark(('bseg', seg), 'sp')
        if main:
            for j in range(CPS):
                r = sm['rtr']
                mx, nmx, zz, rz = r[:, 0:1], r[:, 1:2], r[:, 2:3], r[:, 3:4]
                m1, m2r, den, rd = r[:, 4:5], r[:, 5:6], r[:, 6:7], r[:, 7:8]
                ex, sc, sc2 = sm['ex'][:, 0:8], sm['ex'][:, 8:16], sm['ex'][:, 16:24]
                e1, msk, cwr = sm['ex'][:, 24:32], sm['ex'][:, 32:40], sm['ex'][:, 40:48]
                lg = G[2][:, j * 24 + 16: j * 24 + 24]
                P.on('vector', lambda v: v.reduce_max(mx, lg, axis=AX.X))
                P.on('vector', lambda v: v.tensor_scalar_mul(nmx, mx, -1.0))
                P.on('vector', lambda v: v.tensor_scalar_add(sc2, lg, nmx), inc='vp')
                P.mark(('rsub', seg, j), 'vp')
                P.wmark('scalar', ('rsub', seg, j))
                P.on('scalar', lambda s: s.activation(ex, sc2, AF.Exp), inc='sp')
                P.mark(('rexp', seg, j), 'sp')
                P.wmark('vector', ('rexp', seg, j))
                P.on('vector', lambda v: v.reduce_sum(zz, ex, axis=AX.X))
                P.on('vector', lambda v: v.reciprocal(rz, zz))
                P.on('vector', lambda v: v.tensor_scalar_mul(sc, ex, rz))
                P.on('vector', lambda v: v.reduce_max(m1, sc, axis=AX.X))
                P.on('vector', lambda v: v.tensor_single_scalar(e1, sc, m1, ALU.is_ge))
                P.on('vector', lambda v: v.scalar_tensor_tensor(sc2, e1, -1e30, sc,
                                                                ALU.mult, ALU.add))
                P.on('vector', lambda v: v.reduce_max(m2r, sc2, axis=AX.X))
                P.on('vector', lambda v: v.tensor_single_scalar(msk, sc, m2r, ALU.is_ge))
                P.on('vector', lambda v: v.tensor_mul(cwr, sc, msk))
                P.on('vector', lambda v: v.reduce_sum(den, cwr, axis=AX.X))
                P.on('vector', lambda v: v.tensor_scalar_add(den, den, 1e-20))
                P.on('vector', lambda v: v.reciprocal(rd, den))
                P.on('vector', lambda v, j=j: v.tensor_scalar_mul(
                    sm['cw'][:, j * 8:(j + 1) * 8], cwr, rd), inc='vp')
                P.mark(('cw', seg, j), 'vp')
        P.setguard('B2', 'vp')
        P.mark(('pre-done', seg), 'vp')

        if os.environ.get("KPH") == "pre":
            continue
        P.wmark('tensor', ('xnT-done', seg))
        P.wmark('tensor', ('pre-done', seg))
        P.wmark('tensor', ('wk', 0 if seg < 3 else 3))
        P.wmark('tensor', ('wv', 0 if seg < 3 else 3))
        if main:
            P.wmark('tensor', ('wq', seg))
        for j in range(CPS):
            for fg in range(2):
                bq, bk, bv = (0, 1, 2) if fg == 0 else (3, 4, 5)
                if main:
                    P.wguard('tensor', f'B{bq}')
                P.wguard('tensor', f'B{bk}', f'B{bv}')
                for c in range(DC):
                    xt = ab[:, XNT_O + c * 1024 + j * 128:
                            XNT_O + c * 1024 + j * 128 + 128]
                    st_, sp_ = (c == 0), (c == DC - 1)
                    if main:
                        P.on('tensor', lambda t, xt=xt, c=c, fg=fg, bq=bq, st_=st_, sp_=sp_:
                             t.matmul(G[bq][:], xt, pw(0, c * 1024 + fg * 512, 512),
                                      start=st_, stop=sp_))
                    P.on('tensor', lambda t, xt=xt, c=c, fg=fg, bk=bk, st_=st_, sp_=sp_:
                         t.matmul(G[bk][:], xt, pw(1, c * 1024 + fg * 512, 512),
                                  start=st_, stop=sp_))
                    P.on('tensor', lambda t, xt=xt, c=c, fg=fg, bv=bv, st_=st_, sp_=sp_:
                         t.matmul(G[bv][:], xt, pw(2, c * 1024 + fg * 512, 512),
                                  start=st_, stop=sp_))
                P.on('tensor', lambda t: t.drain(), inc='tp')
                P.mark(('projmm', seg, j, fg), 'tp')

                q2 = sm['qn']
                cb = fg * 32
                kss, ksd = q2[:, cb:cb + 8], q2[:, cb + 8:cb + 16]
                krc, rkb = q2[:, cb + 16:cb + 24], q2[:, cb + 24:cb + 32]
                cb2 = 64 + fg * 24
                qss, qsd, qrc = (q2[:, cb2:cb2 + 8], q2[:, cb2 + 8:cb2 + 16],
                                 q2[:, cb2 + 16:cb2 + 24])
                ksq = af[:, ASF_O: ASF_O + 512]
                qsq = af[:, ASF_O + 512: ASF_O + 1024]
                P.wmark('scalar', ('projmm', seg, j, fg))
                P.wguard('scalar', 'asf')
                if main:
                    P.on('scalar', lambda s: s.activation(ksq, G[bk][:], AF.Square),
                         inc='sp')
                    P.on('scalar', lambda s: s.activation(qsq, G[bq][:], AF.Square),
                         inc='sp')
                else:
                    P.on('scalar', lambda s: s.activation(ksq, G[bk][:], AF.Square),
                         inc='sp')
                P.mark(('projsq', seg, j, fg), 'sp')
                P.wmark('vector', ('projsq', seg, j, fg))
                P.wmark('vector', ('bseg', seg))
                if ('projsd', seg, j - 1, fg) in P.marks:
                    P.wmark('vector', ('projsd', seg, j - 1, fg))
                kview = ksq.rearrange("p (h w) -> p h w", w=64)
                if main:
                    P.on('vector', lambda v: v.reduce_sum(kss.unsqueeze(2), kview,
                                                          axis=AX.X))
                    qview = qsq.rearrange("p (h w) -> p h w", w=64)
                    P.on('vector', lambda v: v.reduce_sum(qss.unsqueeze(2), qview,
                                                          axis=AX.X), inc='vp')
                else:
                    P.on('vector', lambda v: v.reduce_sum(kss.unsqueeze(2), kview,
                                                          axis=AX.X), inc='vp')
                P.mark(('projss', seg, j, fg), 'vp')
                P.setguard('asf', 'vp')
                P.wmark('scalar', ('projss', seg, j, fg))
                if main:
                    P.on('scalar', lambda s: s.activation(ksd, kss, AF.Sqrt))
                    P.on('scalar', lambda s: s.activation(qsd, qss, AF.Sqrt), inc='sp')
                else:
                    P.on('scalar', lambda s: s.activation(ksd, kss, AF.Sqrt), inc='sp')
                P.mark(('projsd', seg, j, fg), 'sp')
                P.wmark('vector', ('projsd', seg, j, fg))
                P.on('vector', lambda v: v.tensor_scalar_max(ksd, ksd, 1e-12))
                P.on('vector', lambda v: v.reciprocal(krc, ksd))
                bslice = sm['bseg'][:, j * 16 + fg * 8: j * 16 + fg * 8 + 8]
                P.on('vector', lambda v: v.tensor_mul(rkb, krc, bslice))
                ktm_dst = ab[:, KTM_O + j * 1024 + fg * 512:
                             KTM_O + j * 1024 + fg * 512 + 512]
                rkb_b = rkb.unsqueeze(2).broadcast_to([128, 8, 64])
                P.on('vector', lambda v: v.tensor_tensor(
                    ktm_dst.rearrange("p (h w) -> p h w", w=64),
                    G[bk][:].rearrange("p (h w) -> p h w", w=64), rkb_b, ALU.mult),
                    inc='vp' if not main else None)
                if main:
                    P.on('vector', lambda v: v.tensor_scalar_max(qsd, qsd, 1e-12))
                    P.on('vector', lambda v: v.reciprocal(qrc, qsd))
                    qtm_dst = ab[:, S16_O + j * 1024 + fg * 512:
                                 S16_O + j * 1024 + fg * 512 + 512]
                    qrc_b = qrc.unsqueeze(2).broadcast_to([128, 8, 64])
                    P.on('vector', lambda v: v.tensor_tensor(
                        qtm_dst.rearrange("p (h w) -> p h w", w=64),
                        G[bq][:].rearrange("p (h w) -> p h w", w=64), qrc_b, ALU.mult),
                        inc='vp')
                P.mark(('projepi', seg, j, fg), 'vp')
                P.setguard(f'B{bk}', 'vp')
                if main:
                    P.setguard(f'B{bq}', 'vp')
                P.wmark('scalar', ('projmm', seg, j, fg))
                vtm_dst = ab[:, VTM_O + j * 1024 + fg * 512:
                             VTM_O + j * 1024 + fg * 512 + 512]
                P.on('scalar', lambda s: s.copy(vtm_dst, G[bv][:]), inc='sp')
                P.mark(('vevac', seg, j, fg), 'sp')
                P.setguard(f'B{bv}', 'sp')
        P.mark(('proj-done', seg), 'vp')
        P.mark(('proj-done-s', seg), 'sp')
        P.setguard('pw1', 'tp')
        P.setguard('pw2', 'tp')
        if main:
            P.setguard('pw0', 'tp')

        if os.environ.get("KPH") == "proj":
            continue
        if not main:
            for j in range(CPS):
                g = seg * CPS + j
                P.wmark('tensor', ('projepi', seg, j, 0))
                P.wmark('tensor', ('projepi', seg, j, 1))
                P.wmark('tensor', ('vevac', seg, j, 0))
                P.wmark('tensor', ('vevac', seg, j, 1))
                for h in range(H):
                    kt = ab[:, KTM_O + j * 1024 + h * 64: KTM_O + j * 1024 + h * 64 + 64]
                    vt = ab[:, VTM_O + j * 1024 + h * 64: VTM_O + j * 1024 + h * 64 + 64]
                    sbk = SP[h % 2]
                    off = (h // 2) * 64
                    par = (h % 2) * 64
                    first = (g == 0 and h < 2)
                    P.on('tensor', lambda t, kt=kt, vt=vt, sbk=sbk, off=off, par=par,
                         first=first:
                         t.matmul(sbk[par:par + 64, off:off + 64], kt, vt,
                                  start=first, stop=False, tile_position=(0, par)))
                P.on('tensor', lambda t: t.drain(), inc='tp')
                P.mark(('state', g), 'tp')
            if seg == 1:
                P.wmark('scalar', ('state', 15))
                P.on('scalar', lambda s: s.copy(sm['S'][0:64, :], SP[0][0:64, 0:512]),
                     inc='sp')
                P.on('scalar', lambda s: s.copy(sm['S'][64:128, :], SP[1][64:128, 0:512]),
                     inc='sp')
                P.mark(('sevac', 15), 'sp')
            continue

        P.wmark('tensor', ('proj-done', seg))
        for j in range(CPS):
            for src_o, dst_o, nm in ((KTM_O, KT_O, 'k'), (S16_O, QT_O, 'q')):
                for fg2 in range(2):
                    bank = (j * 2 + fg2) % 2
                    P.wguard('tensor', f'B{bank}')
                    for k in range(4):
                        fc = fg2 * 4 + k
                        src = ab[:, src_o + j * 1024 + fc * 128:
                                 src_o + j * 1024 + fc * 128 + 128]
                        P.on('tensor', lambda t, k=k, bank=bank, src=src:
                             t.transpose(Gb[bank][:, k * 128:(k + 1) * 128], src,
                                         sm['ident'][:]),
                             )
                    P.on('tensor', lambda t: t.drain(), inc='tp')
                    P.mark(('pt', seg, j, nm, fg2), 'tp')
                    P.wmark('scalar', ('pt', seg, j, nm, fg2))
                    dst = packed_dst(dst_o + fg2 * 4 * 1024, j)
                    P.on('scalar', lambda s, bank=bank, dst=dst:
                         s.copy(dst, Gb[bank][:, 0:512].rearrange("p (c r) -> p c r", c=4)),
                         inc='sp')
                    P.mark(('ptc', seg, j, nm, fg2), 'sp')
                    P.setguard(f'B{bank}', 'sp')
        P.mark(('kTqT-done', seg), 'sp')

        P.wguard('gpsimd', 'pw0')
        P.on('gpsimd', lambda g: g.dma_start(
            out=pw(0, 0, 8192).rearrange("p (c f) -> p c f", c=DC),
            in_=A['d_wo'].rearrange("(c p) f -> p c f", p=128)), inc='pw0', n=16)
        P.mark(('wo', seg), 'pw0')

        scb = lambda h: (h % 2) + 2 * (h // 8)
        scpos = lambda h: (h % 8) // 2
        P.wmark('tensor', ('kTqT-done', seg))
        P.wmark('vector', 'maskt')
        for j in range(CPS):
            g = 16 + (seg - 2) * CPS + j
            P.wguard('tensor', 'B0', 'B1', 'B2', 'B3')
            for h in range(H):
                par = (h % 2) * 64
                fc = h // 2
                kT = ab[par:par + 64, KT_O + fc * 1024 + j * 128:
                        KT_O + fc * 1024 + j * 128 + 128]
                qT = ab[par:par + 64, QT_O + fc * 1024 + j * 128:
                        QT_O + fc * 1024 + j * 128 + 128]
                bk = scb(h)
                po = scpos(h) * 128
                first = not any(scb(hh) == bk for hh in range(h))
                P.on('tensor', lambda t, kT=kT, qT=qT, bk=bk, po=po, first=first:
                     t.matmul(G[bk][:, po:po + 128], kT, qT, start=first, stop=True))
            P.on('tensor', lambda t: t.drain(), inc='tp')
            P.mark(('sc', g), 'tp')
            P.wguard('tensor', 'SPB')
            for h in range(H):
                kt = ab[:, KTM_O + j * 1024 + h * 64: KTM_O + j * 1024 + h * 64 + 64]
                vt = ab[:, VTM_O + j * 1024 + h * 64: VTM_O + j * 1024 + h * 64 + 64]
                sbk = SP[h % 2]
                off = (h // 2) * 64
                par = (h % 2) * 64
                P.on('tensor', lambda t, kt=kt, vt=vt, sbk=sbk, off=off, par=par:
                     t.matmul(sbk[par:par + 64, off:off + 64], kt, vt,
                              start=False, stop=False, tile_position=(0, par)))
            P.on('tensor', lambda t: t.drain(), inc='tp')
            P.mark(('state', g), 'tp')
            P.wmark('vector', ('sc', g))
            for bk in range(4):
                P.on('vector', lambda v, bk=bk:
                     v.tensor_tensor(ab[:, MSC_O + bk * 512: MSC_O + (bk + 1) * 512],
                                     G[bk][:], sm['maskt'][:], ALU.mult), inc='vp')
                P.setguard(f'B{bk}', 'vp')
            P.mark(('msc', g), 'vp')
            P.wmark('tensor', ('msc', g))
            P.wmark('tensor', ('sevac', g - 1))
            for hh in range(2):
                P.wguard('tensor', 'B4', 'B5')
                for h in range(hh * 8, hh * 8 + 8):
                    ob = 4 + (h % 2)
                    par = (h % 2) * 64
                    po = ((h % 8) // 2) * 128
                    vt = ab[:, VTM_O + j * 1024 + h * 64: VTM_O + j * 1024 + h * 64 + 64]
                    ms = ab[:, MSC_O + scb(h) * 512 + scpos(h) * 128:
                            MSC_O + scb(h) * 512 + scpos(h) * 128 + 128]
                    first = (h - hh * 8) < 2
                    P.on('tensor', lambda t, vt=vt, ms=ms, ob=ob, par=par, po=po,
                         first=first:
                         t.matmul(G[ob][par:par + 64, po:po + 128], vt, ms,
                                  start=first, stop=False, tile_position=(0, par)))
                    Ssl = sm['S'][par:par + 64, (h // 2) * 64: (h // 2) * 64 + 64]
                    fc = h // 2
                    qT = ab[par:par + 64, QT_O + fc * 1024 + j * 128:
                            QT_O + fc * 1024 + j * 128 + 128]
                    P.on('tensor', lambda t, Ssl=Ssl, qT=qT, ob=ob, par=par, po=po:
                         t.matmul(G[ob][par:par + 64, po:po + 128], Ssl, qT,
                                  start=False, stop=True, tile_position=(par, par)))
                P.on('tensor', lambda t: t.drain(), inc='tp')
                P.mark(('omm', g, hh), 'tp')
                P.wmark('scalar', ('omm', g, hh))
                src_e = G[4][0:64, 0:512].rearrange("p (c r) -> p c r", c=4)
                dst_e = packed_dst(PW_O + 8192 + hh * 4 * 1024, j)[0:64]
                P.on('scalar', lambda s, src_e=src_e, dst_e=dst_e:
                     s.copy(dst_e, src_e), inc='sp')
                src_o2 = G[5][64:128, 0:512].rearrange("p (c r) -> p c r", c=4)
                dst_o2 = packed_dst(PW_O + 8192 + hh * 4 * 1024, j)[64:128]
                P.on('scalar', lambda s, src_o2=src_o2, dst_o2=dst_o2:
                     s.copy(dst_o2, src_o2), inc='sp')
                P.setguard('B4', 'sp')
                P.setguard('B5', 'sp')
            P.mark(('omm', g), 'tp')
            P.mark(('oevac', g), 'sp')
            P.wmark('scalar', ('omm', g))
            P.on('scalar', lambda s: s.copy(sm['S'][0:64, :], SP[0][0:64, 0:512]),
                 inc='sp')
            P.on('scalar', lambda s: s.copy(sm['S'][64:128, :], SP[1][64:128, 0:512]),
                 inc='sp')
            P.mark(('sevac', g), 'sp')
            P.setguard('SPB', 'sp')
        P.mark(('attn-done', seg), 'sp')

        P.wmark('tensor', ('attn-done', seg))
        P.wmark('tensor', ('wo', seg))
        for j in range(CPS):
            for dg in range(2):
                yb = (j * 2 + dg) % 2
                P.wguard('tensor', f'B{yb}')
                for fc in range(DC):
                    aoT = ab[:, PW_O + 8192 + fc * 1024 + j * 128:
                             PW_O + 8192 + fc * 1024 + j * 128 + 128]
                    P.on('tensor', lambda t, aoT=aoT, fc=fc, dg=dg, yb=yb:
                         t.matmul(G[yb][:], aoT, pw(0, fc * 1024 + dg * 512, 512),
                                  start=(fc == 0), stop=(fc == DC - 1)))
                P.on('tensor', lambda t: t.drain(), inc='tp')
                P.mark(('wo-mm', seg, j, dg), 'tp')
                P.wmark('scalar', ('wo-mm', seg, j, dg))
                P.wguard('scalar', f'stg{j % 2}')
                stg = ab[:, STG_O + (j % 2) * 1024 + dg * 512:
                         STG_O + (j % 2) * 1024 + dg * 512 + 512]
                P.on('scalar', lambda s, stg=stg, yb=yb: s.copy(stg, G[yb][:]),
                     inc='sp')
                P.setguard(f'B{yb}', 'sp')
            P.mark(('s-stage', seg, j), 'sp')
            P.wmark('sync', ('s-stage', seg, j))
            row = (seg - 2) * 1024 + j * 128
            P.on('sync', lambda s, row=row, j=j: s.dma_start(
                out=A['d_sscr'][row:row + 128, :],
                in_=ab[:, STG_O + (j % 2) * 1024: STG_O + (j % 2) * 1024 + 1024]),
                inc=f'st{j % 2}', n=16)
            P.mark(('s-store', seg, j), f'st{j % 2}')
            P.setguard(f'stg{j % 2}', f'st{j % 2}')
        P.mark(('wo-done', seg), 'tp')
        P.setguard('pw0', 'tp')
        P.setguard('pw1', 'tp')

        if KDBG2 and seg == 3:
            P.wmark('sync', ('wo-done', seg))
            for nm3, off3 in (('dkT', KT_O), ('dqT', QT_O), ('dktm', KTM_O),
                              ('dvtm', VTM_O)):
                for ci in range(8):
                    P.on('sync', lambda s, nm3=nm3, off3=off3, ci=ci: s.dma_start(
                        out=A['d_' + nm3][:, ci * 1024:(ci + 1) * 1024],
                        in_=ab[:, off3 + ci * 1024: off3 + (ci + 1) * 1024]),
                        inc='xc', n=16)
            P.on('sync', lambda s: s.dma_start(out=A['d_dS2'], in_=sm['S'][:]),
                 inc='xc', n=16)
            P.mark(('kdbg2', seg), 'xc')
        for rname in ('w1s0', 'w1s1', 'w3s0', 'w3s1', 'w2h0', 'w2h1'):
            P.guard[rname] = P.marks[('wo-done', seg)]
            if KDBG2 and seg == 3:
                P.guard[rname] = P.marks[('kdbg2', seg)]
        if KDBG2 and seg == 3:
            P.wmark('vector', ('kdbg2', seg))
            P.wmark('gpsimd', ('kdbg2', seg))
        w1s = [ab[:, W13_O + i * 4096: W13_O + (i + 1) * 4096] for i in range(2)]
        w3s = [ab[:, W13_O + 8192 + i * 4096: W13_O + 8192 + (i + 1) * 4096]
               for i in range(2)]
        w2h = [ab[:, W2_O + i * 11264: W2_O + (i + 1) * 11264] for i in range(2)]

        def w13_dma(e2, gi):
            hf0, ntile = GRP[gi]
            sl = gi % 2
            cols = ntile * 128
            for (mat, dram, slots) in (('w1', A['d_w1t'], w1s),
                                       ('w3', A['d_w3t'], w3s)):
                P.wguard('gpsimd', f'{mat}s{sl}')
                dst = slots[sl].rearrange("p (c f) -> p c f", f=512)[:, :, 0:cols]
                src = dram[e2, :, hf0 * 128: hf0 * 128 + cols].rearrange(
                    "(c p) f -> p c f", p=128)
                P.on('gpsimd', lambda g2, dst=dst, src=src:
                     g2.dma_start(out=dst, in_=src), inc=f'{mat}s{sl}', n=16)
                P.mark(('wldr', seg, e2, gi, mat), f'{mat}s{sl}')

        for e in range(E):
            w13_dma(e, 0)
            w13_dma(e, 1)
            for dg in range(2):
                P.wguard('gpsimd', f'w2h{dg}')
                dst = w2h[dg].rearrange("p (c f) -> p c f", f=512)
                src = A['d_w2t'][e, :, dg * 512:(dg + 1) * 512].rearrange(
                    "(c p) f -> p c f", p=128)
                P.on('gpsimd', lambda g2, dst=dst, src=src:
                     g2.dma_start(out=dst, in_=src), inc=f'w2h{dg}', n=16)
                P.mark(('w2ldr', seg, e, dg), f'w2h{dg}')

            if e == 0:
                P.wmark('vector', ('s-store', seg, CPS - 1))
            if KDBG3 and seg == 2 and e == 1:
                P.wmark('sync', ('htt', seg, 0, 21))
                for ci in range(22):
                    P.on('sync', lambda s, ci=ci: s.dma_start(
                        out=A['d_dh'][:, ci * 1024:(ci + 1) * 1024],
                        in_=ab[:, H_O + ci * 1024: H_O + (ci + 1) * 1024]),
                        inc='xc', n=16)
                P.mark(('kdbg3',), 'xc')
                P.wmark('scalar', ('kdbg3',))
                P.wmark('vector', ('kdbg3',))
                P.wmark('tensor', ('kdbg3',))
            for gi, (hf0, ntile) in enumerate(GRP):
                sl = gi % 2
                P.wmark('tensor', ('wldr', seg, e, gi, 'w1'))
                P.wmark('tensor', ('wldr', seg, e, gi, 'w3'))
                for k in range(ntile):
                    hf = hf0 + k
                    P.wguard('tensor', 'B0', 'B1')
                    for c in range(DC):
                        lt = w1s[sl][:, c * 512 + k * 128: c * 512 + k * 128 + 128]
                        for sub in range(2):
                            P.on('tensor', lambda t, lt=lt, c=c, sub=sub:
                                 t.matmul(G[sub][:], lt,
                                          ab[:, XNT_O + c * 1024 + sub * 512:
                                             XNT_O + c * 1024 + sub * 512 + 512],
                                          start=(c == 0), stop=(c == DC - 1)))
                    P.on('tensor', lambda t: t.drain(), inc='tp')
                    P.mark(('a-mm', seg, e, hf), 'tp')
                    P.wguard('tensor', 'B2', 'B3')
                    for c in range(DC):
                        lt = w3s[sl][:, c * 512 + k * 128: c * 512 + k * 128 + 128]
                        for sub in range(2):
                            P.on('tensor', lambda t, lt=lt, c=c, sub=sub:
                                 t.matmul(G[2 + sub][:], lt,
                                          ab[:, XNT_O + c * 1024 + sub * 512:
                                             XNT_O + c * 1024 + sub * 512 + 512],
                                          start=(c == 0), stop=(c == DC - 1)))
                    P.on('tensor', lambda t: t.drain(), inc='tp')
                    P.mark(('b-mm', seg, e, hf), 'tp')
                    P.wmark('scalar', ('a-mm', seg, e, hf))
                    P.wguard('scalar', 'asil')
                    for sub in range(2):
                        P.on('scalar', lambda s, sub=sub:
                             s.activation(af[:, ASF_O + sub * 512: ASF_O + sub * 512 + 512],
                                          G[sub][:], AF.Silu),
                             inc='sp')
                        P.setguard(f'B{sub}', 'sp')
                    P.mark(('silu', seg, e, hf), 'sp')
                    P.wmark('vector', ('silu', seg, e, hf))
                    P.wmark('vector', ('b-mm', seg, e, hf))
                    for sub in range(2):
                        P.on('vector', lambda v, sub=sub, hf=hf:
                             v.tensor_tensor(ab[:, H_O + hf * 1024 + sub * 512:
                                                H_O + hf * 1024 + sub * 512 + 512],
                                             af[:, ASF_O + sub * 512:
                                                ASF_O + sub * 512 + 512],
                                             G[2 + sub][:], ALU.mult),
                             inc='vp' if sub == 1 else None)
                        P.setguard(f'B{2 + sub}', 'vp')
                    P.mark(('htt', seg, e, hf), 'vp')
                    P.setguard('asil', 'vp')
                P.setguard(f'w1s{sl}', 'tp')
                P.setguard(f'w3s{sl}', 'tp')
                if gi + 2 < len(GRP):
                    w13_dma(e, gi + 2)

            P.wmark('tensor', ('htt', seg, e, 21))
            P.wmark('tensor', ('w2ldr', seg, e, 0))
            P.wmark('tensor', ('w2ldr', seg, e, 1))
            for jj in range(CPS):
                for dg in range(2):
                    ti = jj * 2 + dg
                    yb = 4 + ti % 2
                    P.wguard('tensor', f'B{yb}')
                    for hf in range(NHFT):
                        hp = ab[:, H_O + hf * 1024 + jj * 128:
                                H_O + hf * 1024 + jj * 128 + 128]
                        P.on('tensor', lambda t, hp=hp, hf=hf, dg=dg, yb=yb:
                             t.matmul(G[yb][:], hp, w2h[dg][:, hf * 512:(hf + 1) * 512],
                                      start=(hf == 0), stop=(hf == NHFT - 1)))
                    P.on('tensor', lambda t: t.drain(), inc='tp')
                    P.mark(('y-mm', seg, e, ti), 'tp')
                    P.wmark('vector', ('y-mm', seg, e, ti))
                    ysl = af[:, Y_O + jj * 1024 + dg * 512: Y_O + jj * 1024 + dg * 512 + 512]
                    cwsl = sm['cw'][:, jj * 8 + e: jj * 8 + e + 1]
                    if e == 0:
                        P.on('vector', lambda v, ysl=ysl, yb=yb, cwsl=cwsl:
                             v.tensor_scalar_mul(ysl, G[yb][:], cwsl), inc='vp')
                    else:
                        P.on('vector', lambda v, ysl=ysl, yb=yb, cwsl=cwsl:
                             v.scalar_tensor_tensor(ysl, G[yb][:], cwsl, ysl,
                                                    ALU.mult, ALU.add), inc='vp')
                    P.setguard(f'B{yb}', 'vp')
                    P.mark(('stt', seg, e, ti), 'vp')
            P.setguard('w2h0', 'tp')
            P.setguard('w2h1', 'tp')
        P.mark(('w2-done', seg), 'tp')
        P.setguard('pw0', 'tp')
        P.setguard('pw1', 'tp')
        P.setguard('pw2', 'tp')

        for jj in range(CPS):
            row = (seg - 2) * 1024 + jj * 128
            P.wmark('gpsimd', ('s-store', seg, jj))
            P.wguard('gpsimd', f'sld{jj % 2}')
            sld = af[:, SLD_O + (jj % 2) * 1024: SLD_O + (jj % 2) * 1024 + 1024]
            P.on('gpsimd', lambda g2, sld=sld, row=row: g2.dma_start(
                out=sld, in_=A['d_sscr'][row:row + 128, :]), inc=f'sld{jj % 2}', n=16)
            P.mark(('s-ld', seg, jj), f'sld{jj % 2}')
            xr = af[:, XR_O + (jj % 2) * 1024: XR_O + (jj % 2) * 1024 + 1024]
            P.wguard('sync', f'xr{jj % 2}')
            P.on('sync', lambda s, xr=xr, row=row: s.dma_start(
                out=xr, in_=A['d_xseq'][2048 + row: 2048 + row + 128, :]),
                inc=f'xr{jj % 2}', n=16)
            P.mark(('comb-x', seg, jj), f'xr{jj % 2}')
            for dg in range(2):
                ysl = af[:, Y_O + jj * 1024 + dg * 512: Y_O + jj * 1024 + dg * 512 + 512]
                P.wmark('scalar', ('stt', seg, E - 1, jj * 2 + dg))
                P.wguard('scalar', 'gs')
                gsc = af[:, GS_O + dg * 512: GS_O + dg * 512 + 512]
                P.on('scalar', lambda s, gsc=gsc, ysl=ysl:
                     s.activation(gsc, ysl, AF.Silu), inc='sp')
                P.mark(('csilu', seg, jj, dg), 'sp')
                P.wmark('vector', ('csilu', seg, jj, dg))
                P.wmark('vector', ('s-ld', seg, jj))
                P.wmark('vector', ('comb-x', seg, jj))
                sgs = af[:, SLD_O + (jj % 2) * 1024 + dg * 512:
                         SLD_O + (jj % 2) * 1024 + dg * 512 + 512]
                xsl = af[:, XR_O + (jj % 2) * 1024 + dg * 512:
                         XR_O + (jj % 2) * 1024 + dg * 512 + 512]
                P.on('vector', lambda v, gsc=gsc, sgs=sgs: v.tensor_mul(gsc, gsc, sgs))
                P.on('vector', lambda v, xsl=xsl, gsc=gsc: v.tensor_add(xsl, gsc, xsl),
                     inc='vp')
            P.setguard('gs', 'vp')
            P.setguard(f'sld{jj % 2}', 'vp')
            P.mark(('comb-v', seg, jj), 'vp')
            P.wmark('sync', ('comb-v', seg, jj))
            P.on('sync', lambda s, xr=xr, row=row: s.dma_start(
                out=A['d_out'][row:row + 128, :], in_=xr), inc=f'ost{jj % 2}', n=16)
            P.mark(('out-store', seg, jj), f'ost{jj % 2}')
            P.setguard(f'xr{jj % 2}', f'ost{jj % 2}')

    if KDBG:
        for s2 in ('tp', 'vp', 'sp', 'sld0', 'sld1'):
            P.w('sync', s2, P.cnt[s2])
        for ci in range(32):
            P.on('sync', lambda s, ci=ci: s.dma_start(
                out=A['d_dab'][:, ci * 1024:(ci + 1) * 1024],
                in_=ab[:, ci * 1024:(ci + 1) * 1024]), inc='xc', n=16)
        for ci in range(8):
            P.on('sync', lambda s, ci=ci: s.dma_start(
                out=A['d_daf'][:, ci * 1024:(ci + 1) * 1024],
                in_=af[:, ci * 1024:(ci + 1) * 1024]), inc='xc', n=16)
        P.on('sync', lambda s: s.dma_start(out=A['d_dsm'], in_=sm['S'][:]), inc='xc', n=16)
        P.on('sync', lambda s: s.dma_start(out=A['d_dcw'], in_=sm['cw'][:]), inc='xc', n=16)
        P.on('sync', lambda s: s.dma_start(out=A['d_dbs'], in_=sm['bseg'][:]), inc='xc', n=16)
        P.on('sync', lambda s: s.dma_start(out=A['d_dqn'], in_=sm['qn'][:]), inc='xc', n=16)
    for s2 in ('st0', 'st1', 'ost0', 'ost1'):
        P.w('sync', s2, P.cnt[s2])


def build_nc():
    nc = bass.Bass("TRN2", target_bir_lowering=False)
    A = {}
    A['d_xseq'] = nc.dram_tensor("xseq", [VIEW, D], F32, kind="ExternalInput").ap()
    for nmW in ('wq', 'wk', 'wv', 'wo'):
        A['d_' + nmW] = nc.dram_tensor(nmW, [D, D], F32, kind="ExternalInput").ap()
    A['d_wb'] = nc.dram_tensor("wb", [D, H], F32, kind="ExternalInput").ap()
    A['d_wgt'] = nc.dram_tensor("wgt", [D, E], F32, kind="ExternalInput").ap()
    A['d_bbb'] = nc.dram_tensor("bbb", [128, H], F32, kind="ExternalInput").ap()
    A['d_w1t'] = nc.dram_tensor("w1t", [E, D, HF], F32, kind="ExternalInput").ap()
    A['d_w3t'] = nc.dram_tensor("w3t", [E, D, HF], F32, kind="ExternalInput").ap()
    A['d_w2t'] = nc.dram_tensor("w2t", [E, HF, D], F32, kind="ExternalInput").ap()
    A['d_maskt'] = nc.dram_tensor("maskt", [128, 512], F32, kind="ExternalInput").ap()
    A['d_ident'] = nc.dram_tensor("identin", [128, 128], F32, kind="ExternalInput").ap()
    A['d_out'] = nc.dram_tensor("out", [2048, D], F32, kind="ExternalOutput").ap()
    A['d_sscr'] = nc.dram_tensor(
        "sscr", [2048, D], BF16,
        kind="ExternalOutput" if (KDBG or KDBG2) else "Internal").ap()
    if KDBG:
        A['d_dab'] = nc.dram_tensor("dab", [128, 32768], BF16, kind="ExternalOutput").ap()
        A['d_daf'] = nc.dram_tensor("daf", [128, 8192], F32, kind="ExternalOutput").ap()
        A['d_dsm'] = nc.dram_tensor("dsm", [128, 512], BF16, kind="ExternalOutput").ap()
        A['d_dcw'] = nc.dram_tensor("dcw", [128, 64], F32, kind="ExternalOutput").ap()
        A['d_dbs'] = nc.dram_tensor("dbs", [128, 128], F32, kind="ExternalOutput").ap()
        A['d_dqn'] = nc.dram_tensor("dqn", [128, 112], F32, kind="ExternalOutput").ap()
    if KDBG3:
        A['d_dh'] = nc.dram_tensor("dh", [128, 22528], BF16, kind="ExternalOutput").ap()
    if KDBG2:
        for nm3 in ('dkT', 'dqT', 'dktm', 'dvtm'):
            A['d_' + nm3] = nc.dram_tensor(nm3, [128, 8192], BF16, kind="ExternalOutput").ap()
        A['d_dS2'] = nc.dram_tensor("dS2", [128, 512], BF16, kind="ExternalOutput").ap()

    ctx = ExitStack()
    ab = ctx.enter_context(nc.sbuf_tensor("ab", [128, AB_COLS], BF16))
    af = ctx.enter_context(nc.sbuf_tensor("af", [128, AF_COLS], F32))
    smh = {}
    for nm2, shape, dt in (('S', [128, 512], BF16), ('maskt', [128, 512], F32),
                           ('ident', [128, 128], BF16), ('bbb', [128, 16], F32),
                           ('wb', [128, 128], BF16), ('wg', [128, 64], BF16),
                           ('stats', [128, 32], F32), ('btmp', [128, 128], F32),
                           ('bseg', [128, 128], F32), ('rtr', [128, 8], F32),
                           ('ex', [128, 48], F32), ('cw', [128, 64], F32),
                           ('qn', [128, 112], F32)):
        smh[nm2] = ctx.enter_context(nc.sbuf_tensor(nm2 + "_sb", shape, dt))
    SPt = [ctx.enter_context(nc.psum_tensor(f"SP{i}", [128, 512], F32)) for i in range(2)]
    Gt = [ctx.enter_context(nc.psum_tensor(f"G{i}", [128, 512], F32)) for i in range(6)]
    sems = {s: ctx.enter_context(nc.semaphore(s)) for s in
            ('wc', 'xc', 'pw0', 'pw1', 'pw2', 'w1s0', 'w1s1', 'w3s0', 'w3s1',
             'w2h0', 'w2h1', 'sld0', 'sld1', 'xr0', 'xr1', 'st0', 'st1',
             'ost0', 'ost1', 'tp', 'vp', 'sp')}

    A['ab'], A['af'] = ab[:], af[:]
    A['sm'] = {k: v[:] for k, v in smh.items()}
    A['G'] = [g[:] for g in Gt]
    A['SP'] = [s[:] for s in SPt]
    A['Gb'] = [A['G'][i].bitcast(BF16) for i in range(2)]

    shared_marks = {}
    passes = {}
    with nc.Block() as blk:
        for which, deco in (('tensor', blk.tensor), ('vector', blk.vector),
                            ('scalar', blk.scalar), ('gpsimd', blk.gpsimd),
                            ('sync', blk.sync)):
            Pp = Pass(nc, which, sems, shared_marks)
            passes[which] = Pp
            deco(lambda eng, Pp=Pp: emit(Pp, A))
    ref = passes['tensor'].cnt
    for w2_, p2 in passes.items():
        assert p2.cnt == ref, (w2_, p2.cnt, ref)
    ctx.close()
    return nc


def kernel(L0_attn_x, L0_attn_Wq, L0_attn_Wk, L0_attn_Wv, L0_attn_Wo,
           L0_attn_Wb, L0_attn_bb, L0_ffn_Wg, L0_ffn_W1, L0_ffn_W2, L0_ffn_W3):
    x = np.ascontiguousarray(np.asarray(L0_attn_x, dtype=np.float32))
    wq = np.ascontiguousarray(np.asarray(L0_attn_Wq, dtype=np.float32))
    wk = np.ascontiguousarray(np.asarray(L0_attn_Wk, dtype=np.float32))
    wv = np.ascontiguousarray(np.asarray(L0_attn_Wv, dtype=np.float32))
    wo = np.ascontiguousarray(np.asarray(L0_attn_Wo, dtype=np.float32))
    wb = np.ascontiguousarray(np.asarray(L0_attn_Wb, dtype=np.float32))
    bb = np.asarray(L0_attn_bb, dtype=np.float32)
    wgt = np.ascontiguousarray(np.asarray(L0_ffn_Wg, dtype=np.float32).T)
    w1t = np.ascontiguousarray(np.asarray(L0_ffn_W1, dtype=np.float32).transpose(0, 2, 1))
    w3t = np.ascontiguousarray(np.asarray(L0_ffn_W3, dtype=np.float32).transpose(0, 2, 1))
    w2t = np.ascontiguousarray(np.asarray(L0_ffn_W2, dtype=np.float32).transpose(0, 2, 1))
    bbb = np.ascontiguousarray(np.broadcast_to(bb[None, :], (128, H)))
    maskt = np.ascontiguousarray(
        np.tile(np.triu(np.ones((128, 128), dtype=np.float32)), (1, 4)))
    ident = np.eye(128, dtype=np.float32)

    zeros = np.zeros((2048, D), dtype=np.float32)
    in_maps = []
    for core in range(8):
        bidx, half = core // 2, core % 2
        if half == 0:
            xseq = np.ascontiguousarray(
                np.concatenate([zeros, x[bidx, 0:2048]], axis=0))
        else:
            xseq = np.ascontiguousarray(x[bidx])
        in_maps.append({
            "xseq": xseq, "wq": wq, "wk": wk, "wv": wv, "wo": wo, "wb": wb,
            "wgt": wgt, "bbb": bbb, "w1t": w1t, "w3t": w3t, "w2t": w2t,
            "maskt": maskt, "identin": ident,
        })

    nc = build_nc()
    res = run_bass_kernel_spmd(nc, in_maps, core_ids=list(range(8)))
    out = np.empty((B, L, D), dtype=np.float32)
    for core in range(8):
        bidx, half = core // 2, core % 2
        out[bidx, half * 2048:(half + 1) * 2048] = res.results[core]["out"]
    if KDBG or KDBG2 or KDBG3:
        import pickle
        with open('/tmp/kdbg.pkl', 'wb') as fh:
            pickle.dump([{k: np.asarray(v) for k, v in r.items()}
                         for r in res.results], fh)
    return out


# version 12
# speedup vs baseline: 1.0867x; 1.0242x over previous
import os
import numpy as np
from contextlib import ExitStack

KDBG = os.environ.get("KDBG") == "1"
KDBG2 = os.environ.get("KDBG2") == "1"
KDBG3 = os.environ.get("KDBG3") == "1"


import concourse.bass as bass
import concourse.mybir as mybir
from concourse.bass_utils import run_bass_kernel_spmd

F32 = mybir.dt.float32
BF16 = mybir.dt.bfloat16
AF = mybir.ActivationFunctionType
ALU = mybir.AluOpType
AX = mybir.AxisListType

B, L, D = 4, 4096, 1024
H, DH = 16, 64
E, HF = 8, 2816
NHFT = 22
VIEW, SEG, NSEG, CPS, DC = 4096, 1024, 4, 8, 8

XNT_O = 0
S16_O = 8192
KTM_O = 16384
VTM_O = 24576
KT_O = 32768
QT_O = 40960
PW_O = 49152
AB_COLS = 73728
H_O = 8192
W13_O = 32768
W2_O = 49152
MSC_O = S16_O
STG_O = S16_O + 2048

Y_O = 0
XR_O = 8192
GS_O = 10240
SLD_O = 11264
ASF_O = 13312
AF_COLS = 14336

GRP = [(0, 4), (4, 4), (8, 4), (12, 4), (16, 4), (20, 2)]


class Pass:
    def __init__(self, nc, which, sems, shared_marks):
        self.nc = nc
        self.which = which
        self.sems = sems
        self.cnt = {k: 0 for k in sems}
        self.marks = shared_marks
        self.lastw = {}
        self.guard = {}

    def on(self, eng, fn, inc=None, n=1):
        if eng == 'vector':
            self.w('vector', 'vp', self.cnt['vp'])
            if inc is None:
                inc, n = 'vp', 1
        if self.which == eng:
            inst = fn(getattr(self.nc, eng))
            if inc is not None:
                inst.then_inc(self.sems[inc], n)
        if inc is not None:
            self.cnt[inc] += n

    def w(self, eng, sem, val):
        if val <= 0:
            return
        key = (eng, sem)
        if self.lastw.get(key, -1) >= val:
            return
        self.lastw[key] = val
        if self.which == eng:
            getattr(self.nc, eng).wait_ge(self.sems[sem], val)

    def mark(self, key, sem):
        v = (sem, self.cnt[sem])
        if key in self.marks:
            assert self.marks[key] == v, (key, self.marks[key], v, self.which)
        else:
            self.marks[key] = v
        return v

    def wmark(self, eng, key):
        sem, v = self.marks[key]
        self.w(eng, sem, v)

    def wguard(self, eng, *res):
        for r in res:
            g = self.guard.get(r)
            if g is not None:
                self.w(eng, g[0], g[1])

    def setguard(self, res, sem):
        self.guard[res] = (sem, self.cnt[sem])


def emit(P, A):
    ab, af = A["ab"], A["af"]
    G, SP, Gb, sm = A["G"], A["SP"], A["Gb"], A["sm"]

    def pw(i, off, n):
        return ab[:, PW_O + i * 8192 + off: PW_O + i * 8192 + off + n]

    def packed_dst(base, j, r=1024):
        st = base + j * 128
        return ab[:, st: st + 4 * r].rearrange("p (c r) -> p c r", r=r)[:, :, 0:128]

    P.on('gpsimd', lambda g: g.dma_start(out=sm['ident'][:], in_=A['d_ident']),
         inc='wc', n=16)
    P.on('gpsimd', lambda g: g.dma_start(
        out=sm['wb'][:].rearrange("p (c f) -> p c f", c=DC),
        in_=A['d_wb'].rearrange("(c p) f -> p c f", p=128)), inc='wc', n=16)
    P.on('gpsimd', lambda g: g.dma_start(
        out=sm['wg'][:].rearrange("p (c f) -> p c f", c=DC),
        in_=A['d_wgt'].rearrange("(c p) f -> p c f", p=128)), inc='wc', n=16)
    for k2 in ('ident', 'wb', 'wg'):
        P.mark(k2, 'wc')
    P.on('sync', lambda s: s.dma_start(out=sm['maskt'][:], in_=A['d_maskt']),
         inc='xc', n=16)
    P.on('sync', lambda s: s.dma_start(out=sm['bbb'][:], in_=A['d_bbb']),
         inc='xc', n=16)
    for k2 in ('maskt', 'bbb'):
        P.mark(k2, 'xc')

    nseg_emit = int(os.environ.get("KSEGS", str(NSEG)))
    for seg in range(nseg_emit):
        main = seg >= 2
        tok0 = seg * SEG

        if seg in (0, 3):
            P.wguard('gpsimd', 'pw1')
            P.on('gpsimd', lambda g: g.dma_start(
                out=pw(1, 0, 8192).rearrange("p (c f) -> p c f", c=DC),
                in_=A['d_wk'].rearrange("(c p) f -> p c f", p=128)), inc='pw1', n=16)
            P.mark(('wk', seg), 'pw1')
            P.wguard('gpsimd', 'pw2')
            P.on('gpsimd', lambda g: g.dma_start(
                out=pw(2, 0, 8192).rearrange("p (c f) -> p c f", c=DC),
                in_=A['d_wv'].rearrange("(c p) f -> p c f", p=128)), inc='pw2', n=16)
            P.mark(('wv', seg), 'pw2')
        if main:
            P.wguard('gpsimd', 'pw0')
            P.on('gpsimd', lambda g: g.dma_start(
                out=pw(0, 0, 8192).rearrange("p (c f) -> p c f", c=DC),
                in_=A['d_wq'].rearrange("(c p) f -> p c f", p=128)), inc='pw0', n=16)
            P.mark(('wq', seg), 'pw0')

        for j in range(CPS):
            xr = af[:, XR_O + (j % 2) * 1024: XR_O + (j % 2) * 1024 + 1024]
            P.wguard('sync', f'xr{j % 2}')
            P.on('sync', lambda s: s.dma_start(
                out=xr, in_=A['d_xseq'][tok0 + j * 128: tok0 + (j + 1) * 128, :]),
                inc=f'xr{j % 2}', n=16)
            P.mark(('lnx', seg, j), f'xr{j % 2}')

            st = sm['stats']
            c0 = (j % 2) * 16
            msum, mneg = st[:, c0:c0 + 1], st[:, c0 + 2:c0 + 3]
            ssum, rstd = st[:, c0 + 1:c0 + 2], st[:, c0 + 3:c0 + 4]
            ex2, m2 = st[:, c0 + 4:c0 + 5], st[:, c0 + 5:c0 + 6]
            var, sd = st[:, c0 + 6:c0 + 7], st[:, c0 + 7:c0 + 8]
            P.wmark('vector', ('lnx', seg, j))
            P.wmark('scalar', ('lnx', seg, j))
            P.on('vector', lambda v: v.reduce_sum(msum, xr, axis=AX.X))
            P.on('vector', lambda v: v.tensor_scalar_mul(mneg, msum, -1.0 / 1024.0))
            sqd = af[:, GS_O:GS_O + 1024]
            P.wguard('scalar', 'gs')
            P.on('scalar', lambda s: s.activation(sqd, xr, AF.Square, accum_out=ssum),
                 inc='sp')
            P.mark(('lnsq', seg, j), 'sp')
            P.wmark('vector', ('lnsq', seg, j))
            P.on('vector', lambda v: v.tensor_scalar_mul(ex2, ssum, 1.0 / 1024.0))
            P.on('vector', lambda v: v.tensor_mul(m2, mneg, mneg))
            P.on('vector', lambda v: v.tensor_sub(var, ex2, m2))
            P.on('vector', lambda v: v.tensor_scalar_add(var, var, 1e-5), inc='vp')
            P.mark(('lnvar', seg, j), 'vp')
            P.wmark('scalar', ('lnvar', seg, j))
            P.on('scalar', lambda s: s.activation(sd, var, AF.Sqrt),
                 inc='sp')
            P.mark(('lnsd', seg, j), 'sp')
            P.wmark('vector', ('lnsd', seg, j))
            P.on('vector', lambda v: v.reciprocal(rstd, sd))
            xn = ab[:, S16_O + j * 1024: S16_O + (j + 1) * 1024]
            P.on('vector', lambda v: v.tensor_scalar(xn, xr, mneg, rstd,
                                                     ALU.add, ALU.mult), inc='vp')
            P.mark(('lnap', seg, j), 'vp')
            P.setguard(f'xr{j % 2}', 'vp')

        P.wmark('tensor', 'ident')
        for j in range(CPS):
            P.wmark('tensor', ('lnap', seg, j))
            for fg2 in range(2):
                bank = (j * 2 + fg2) % 2
                P.wguard('tensor', f'B{bank}')
                for k in range(4):
                    fc = fg2 * 4 + k
                    src = ab[:, S16_O + j * 1024 + fc * 128:
                             S16_O + j * 1024 + fc * 128 + 128]
                    P.on('tensor', lambda t, k=k, bank=bank, src=src:
                         t.transpose(Gb[bank][:, k * 128:(k + 1) * 128], src,
                                     sm['ident'][:]),
                         )
                P.on('tensor', lambda t: t.drain(), inc='tp')
                P.mark(('xt', seg, j, fg2), 'tp')
                P.wmark('scalar', ('xt', seg, j, fg2))
                dst = packed_dst(XNT_O + fg2 * 4 * 1024, j)
                P.on('scalar', lambda s, bank=bank, dst=dst:
                     s.copy(dst, Gb[bank][:, 0:512].rearrange("p (c r) -> p c r", c=4)),
                     inc='sp')
                P.mark(('xnT', seg, j, fg2), 'sp')
                P.setguard(f'B{bank}', 'sp')
        P.mark(('xnT-done', seg), 'sp')

        if os.environ.get("KPH") == "ln":
            continue
        P.wmark('tensor', 'wb')
        if main:
            P.wmark('tensor', 'wg')
        P.wguard('tensor', 'B2')
        for j in range(CPS):
            P.wmark('tensor', ('xnT', seg, j, 0))
            P.wmark('tensor', ('xnT', seg, j, 1))
            for c in range(DC):
                xt = ab[:, XNT_O + c * 1024 + j * 128: XNT_O + c * 1024 + j * 128 + 128]
                first = (j == 0 and c == 0)
                last = (c == DC - 1)
                if main:
                    P.on('tensor', lambda t, xt=xt, j=j, first=first, last=last:
                         t.matmul(G[2][:, j * 24: j * 24 + 16], xt,
                                  sm['wb'][:, c * 16:(c + 1) * 16],
                                  start=first, stop=last))
                    P.on('tensor', lambda t, xt=xt, j=j, last=last:
                         t.matmul(G[2][:, j * 24 + 16: j * 24 + 24], xt,
                                  sm['wg'][:, c * 8:(c + 1) * 8],
                                  start=False, stop=last))
                else:
                    P.on('tensor', lambda t, xt=xt, j=j, first=first, last=last:
                         t.matmul(G[2][:, j * 24: j * 24 + 16], xt,
                                  sm['wb'][:, c * 16:(c + 1) * 16],
                                  start=first, stop=last))
            P.on('tensor', lambda t: t.drain(), inc='tp')
            P.mark(('pre', seg, j), 'tp')
        P.wmark('vector', ('pre', seg, CPS - 1))
        P.wmark('vector', 'bbb')
        bsl = G[2][:, 0:192].rearrange("p (j f) -> p j f", j=8)[:, :, 0:16]
        bbb_b = sm['bbb'][:].unsqueeze(1).broadcast_to([128, 8, 16])
        P.on('vector', lambda v: v.tensor_tensor(
            sm['btmp'][:].rearrange("p (j f) -> p j f", j=8), bsl, bbb_b, ALU.add),
            inc='vp')
        P.mark(('btmp', seg), 'vp')
        P.wmark('scalar', ('btmp', seg))
        P.on('scalar', lambda s: s.activation(sm['bseg'][:], sm['btmp'][:], AF.Sigmoid),
             inc='sp')
        P.mark(('bseg', seg), 'sp')
        if main:
            for j in range(CPS):
                r = sm['rtr']
                mx, nmx, zz, rz = r[:, 0:1], r[:, 1:2], r[:, 2:3], r[:, 3:4]
                m1, m2r, den, rd = r[:, 4:5], r[:, 5:6], r[:, 6:7], r[:, 7:8]
                ex, sc, sc2 = sm['ex'][:, 0:8], sm['ex'][:, 8:16], sm['ex'][:, 16:24]
                e1, msk, cwr = sm['ex'][:, 24:32], sm['ex'][:, 32:40], sm['ex'][:, 40:48]
                lg = G[2][:, j * 24 + 16: j * 24 + 24]
                P.on('vector', lambda v: v.reduce_max(mx, lg, axis=AX.X))
                P.on('vector', lambda v: v.tensor_scalar_mul(nmx, mx, -1.0))
                P.on('vector', lambda v: v.tensor_scalar_add(sc2, lg, nmx), inc='vp')
                P.mark(('rsub', seg, j), 'vp')
                P.wmark('scalar', ('rsub', seg, j))
                P.on('scalar', lambda s: s.activation(ex, sc2, AF.Exp), inc='sp')
                P.mark(('rexp', seg, j), 'sp')
                P.wmark('vector', ('rexp', seg, j))
                P.on('vector', lambda v: v.reduce_sum(zz, ex, axis=AX.X))
                P.on('vector', lambda v: v.reciprocal(rz, zz))
                P.on('vector', lambda v: v.tensor_scalar_mul(sc, ex, rz))
                P.on('vector', lambda v: v.reduce_max(m1, sc, axis=AX.X))
                P.on('vector', lambda v: v.tensor_single_scalar(e1, sc, m1, ALU.is_ge))
                P.on('vector', lambda v: v.scalar_tensor_tensor(sc2, e1, -1e30, sc,
                                                                ALU.mult, ALU.add))
                P.on('vector', lambda v: v.reduce_max(m2r, sc2, axis=AX.X))
                P.on('vector', lambda v: v.tensor_single_scalar(msk, sc, m2r, ALU.is_ge))
                P.on('vector', lambda v: v.tensor_mul(cwr, sc, msk))
                P.on('vector', lambda v: v.reduce_sum(den, cwr, axis=AX.X))
                P.on('vector', lambda v: v.tensor_scalar_add(den, den, 1e-20))
                P.on('vector', lambda v: v.reciprocal(rd, den))
                P.on('vector', lambda v, j=j: v.tensor_scalar_mul(
                    sm['cw'][:, j * 8:(j + 1) * 8], cwr, rd), inc='vp')
                P.mark(('cw', seg, j), 'vp')
        P.setguard('B2', 'vp')
        P.mark(('pre-done', seg), 'vp')

        if os.environ.get("KPH") == "pre":
            continue
        P.wmark('tensor', ('xnT-done', seg))
        P.wmark('tensor', ('pre-done', seg))
        P.wmark('tensor', ('wk', 0 if seg < 3 else 3))
        P.wmark('tensor', ('wv', 0 if seg < 3 else 3))
        if main:
            P.wmark('tensor', ('wq', seg))
        for j in range(CPS):
            for fg in range(2):
                bq, bk, bv = (0, 1, 2) if fg == 0 else (3, 4, 5)
                if main:
                    P.wguard('tensor', f'B{bq}')
                P.wguard('tensor', f'B{bk}', f'B{bv}')
                for c in range(DC):
                    xt = ab[:, XNT_O + c * 1024 + j * 128:
                            XNT_O + c * 1024 + j * 128 + 128]
                    st_, sp_ = (c == 0), (c == DC - 1)
                    if main:
                        P.on('tensor', lambda t, xt=xt, c=c, fg=fg, bq=bq, st_=st_, sp_=sp_:
                             t.matmul(G[bq][:], xt, pw(0, c * 1024 + fg * 512, 512),
                                      start=st_, stop=sp_))
                    P.on('tensor', lambda t, xt=xt, c=c, fg=fg, bk=bk, st_=st_, sp_=sp_:
                         t.matmul(G[bk][:], xt, pw(1, c * 1024 + fg * 512, 512),
                                  start=st_, stop=sp_))
                    P.on('tensor', lambda t, xt=xt, c=c, fg=fg, bv=bv, st_=st_, sp_=sp_:
                         t.matmul(G[bv][:], xt, pw(2, c * 1024 + fg * 512, 512),
                                  start=st_, stop=sp_))
                P.on('tensor', lambda t: t.drain(), inc='tp')
                P.mark(('projmm', seg, j, fg), 'tp')

                q2 = sm['qn']
                cb = fg * 32
                kss, ksd = q2[:, cb:cb + 8], q2[:, cb + 8:cb + 16]
                krc, rkb = q2[:, cb + 16:cb + 24], q2[:, cb + 24:cb + 32]
                cb2 = 64 + fg * 24
                qss, qsd, qrc = (q2[:, cb2:cb2 + 8], q2[:, cb2 + 8:cb2 + 16],
                                 q2[:, cb2 + 16:cb2 + 24])
                ksq = af[:, ASF_O: ASF_O + 512]
                qsq = af[:, ASF_O + 512: ASF_O + 1024]
                P.wmark('scalar', ('projmm', seg, j, fg))
                P.wguard('scalar', 'asf')
                if main:
                    P.on('scalar', lambda s: s.activation(ksq, G[bk][:], AF.Square),
                         inc='sp')
                    P.on('scalar', lambda s: s.activation(qsq, G[bq][:], AF.Square),
                         inc='sp')
                else:
                    P.on('scalar', lambda s: s.activation(ksq, G[bk][:], AF.Square),
                         inc='sp')
                P.mark(('projsq', seg, j, fg), 'sp')
                P.wmark('vector', ('projsq', seg, j, fg))
                P.wmark('vector', ('bseg', seg))
                if ('projsd', seg, j - 1, fg) in P.marks:
                    P.wmark('vector', ('projsd', seg, j - 1, fg))
                kview = ksq.rearrange("p (h w) -> p h w", w=64)
                if main:
                    P.on('vector', lambda v: v.reduce_sum(kss.unsqueeze(2), kview,
                                                          axis=AX.X))
                    qview = qsq.rearrange("p (h w) -> p h w", w=64)
                    P.on('vector', lambda v: v.reduce_sum(qss.unsqueeze(2), qview,
                                                          axis=AX.X), inc='vp')
                else:
                    P.on('vector', lambda v: v.reduce_sum(kss.unsqueeze(2), kview,
                                                          axis=AX.X), inc='vp')
                P.mark(('projss', seg, j, fg), 'vp')
                P.setguard('asf', 'vp')
                P.wmark('scalar', ('projss', seg, j, fg))
                if main:
                    P.on('scalar', lambda s: s.activation(ksd, kss, AF.Sqrt))
                    P.on('scalar', lambda s: s.activation(qsd, qss, AF.Sqrt), inc='sp')
                else:
                    P.on('scalar', lambda s: s.activation(ksd, kss, AF.Sqrt), inc='sp')
                P.mark(('projsd', seg, j, fg), 'sp')
                P.wmark('vector', ('projsd', seg, j, fg))
                P.on('vector', lambda v: v.tensor_scalar_max(ksd, ksd, 1e-12))
                P.on('vector', lambda v: v.reciprocal(krc, ksd))
                bslice = sm['bseg'][:, j * 16 + fg * 8: j * 16 + fg * 8 + 8]
                P.on('vector', lambda v: v.tensor_mul(rkb, krc, bslice))
                ktm_dst = ab[:, KTM_O + j * 1024 + fg * 512:
                             KTM_O + j * 1024 + fg * 512 + 512]
                rkb_b = rkb.unsqueeze(2).broadcast_to([128, 8, 64])
                P.on('vector', lambda v: v.tensor_tensor(
                    ktm_dst.rearrange("p (h w) -> p h w", w=64),
                    G[bk][:].rearrange("p (h w) -> p h w", w=64), rkb_b, ALU.mult),
                    inc='vp' if not main else None)
                if main:
                    P.on('vector', lambda v: v.tensor_scalar_max(qsd, qsd, 1e-12))
                    P.on('vector', lambda v: v.reciprocal(qrc, qsd))
                    qtm_dst = ab[:, S16_O + j * 1024 + fg * 512:
                                 S16_O + j * 1024 + fg * 512 + 512]
                    qrc_b = qrc.unsqueeze(2).broadcast_to([128, 8, 64])
                    P.on('vector', lambda v: v.tensor_tensor(
                        qtm_dst.rearrange("p (h w) -> p h w", w=64),
                        G[bq][:].rearrange("p (h w) -> p h w", w=64), qrc_b, ALU.mult),
                        inc='vp')
                P.mark(('projepi', seg, j, fg), 'vp')
                P.setguard(f'B{bk}', 'vp')
                if main:
                    P.setguard(f'B{bq}', 'vp')
                P.wmark('scalar', ('projmm', seg, j, fg))
                vtm_dst = ab[:, VTM_O + j * 1024 + fg * 512:
                             VTM_O + j * 1024 + fg * 512 + 512]
                P.on('scalar', lambda s: s.copy(vtm_dst, G[bv][:]), inc='sp')
                P.mark(('vevac', seg, j, fg), 'sp')
                P.setguard(f'B{bv}', 'sp')
        P.mark(('proj-done', seg), 'vp')
        P.mark(('proj-done-s', seg), 'sp')
        P.setguard('pw1', 'tp')
        P.setguard('pw2', 'tp')
        if main:
            P.setguard('pw0', 'tp')

        if os.environ.get("KPH") == "proj":
            continue
        if not main:
            for j in range(CPS):
                g = seg * CPS + j
                P.wmark('tensor', ('projepi', seg, j, 0))
                P.wmark('tensor', ('projepi', seg, j, 1))
                P.wmark('tensor', ('vevac', seg, j, 0))
                P.wmark('tensor', ('vevac', seg, j, 1))
                for h in range(H):
                    kt = ab[:, KTM_O + j * 1024 + h * 64: KTM_O + j * 1024 + h * 64 + 64]
                    vt = ab[:, VTM_O + j * 1024 + h * 64: VTM_O + j * 1024 + h * 64 + 64]
                    sbk = SP[h % 2]
                    off = (h // 2) * 64
                    par = (h % 2) * 64
                    first = (g == 0 and h < 2)
                    P.on('tensor', lambda t, kt=kt, vt=vt, sbk=sbk, off=off, par=par,
                         first=first:
                         t.matmul(sbk[par:par + 64, off:off + 64], kt, vt,
                                  start=first, stop=False, tile_position=(0, par)))
                P.on('tensor', lambda t: t.drain(), inc='tp')
                P.mark(('state', g), 'tp')
            if seg == 1:
                P.wmark('scalar', ('state', 15))
                P.on('scalar', lambda s: s.copy(sm['S'][0:64, :], SP[0][0:64, 0:512]),
                     inc='sp')
                P.on('scalar', lambda s: s.copy(sm['S'][64:128, :], SP[1][64:128, 0:512]),
                     inc='sp')
                P.mark(('sevac', 15), 'sp')
            continue

        P.wmark('tensor', ('proj-done', seg))
        for j in range(CPS):
            for src_o, dst_o, nm in ((KTM_O, KT_O, 'k'), (S16_O, QT_O, 'q')):
                for fg2 in range(2):
                    bank = (j * 2 + fg2) % 2
                    P.wguard('tensor', f'B{bank}')
                    for k in range(4):
                        fc = fg2 * 4 + k
                        src = ab[:, src_o + j * 1024 + fc * 128:
                                 src_o + j * 1024 + fc * 128 + 128]
                        P.on('tensor', lambda t, k=k, bank=bank, src=src:
                             t.transpose(Gb[bank][:, k * 128:(k + 1) * 128], src,
                                         sm['ident'][:]),
                             )
                    P.on('tensor', lambda t: t.drain(), inc='tp')
                    P.mark(('pt', seg, j, nm, fg2), 'tp')
                    P.wmark('scalar', ('pt', seg, j, nm, fg2))
                    dst = packed_dst(dst_o + fg2 * 4 * 1024, j)
                    P.on('scalar', lambda s, bank=bank, dst=dst:
                         s.copy(dst, Gb[bank][:, 0:512].rearrange("p (c r) -> p c r", c=4)),
                         inc='sp')
                    P.mark(('ptc', seg, j, nm, fg2), 'sp')
                    P.setguard(f'B{bank}', 'sp')
        P.mark(('kTqT-done', seg), 'sp')

        P.wguard('gpsimd', 'pw0')
        P.on('gpsimd', lambda g: g.dma_start(
            out=pw(0, 0, 8192).rearrange("p (c f) -> p c f", c=DC),
            in_=A['d_wo'].rearrange("(c p) f -> p c f", p=128)), inc='pw0', n=16)
        P.mark(('wo', seg), 'pw0')

        scb = lambda h: (h % 2) + 2 * (h // 8)
        scpos = lambda h: (h % 8) // 2
        P.wmark('tensor', ('kTqT-done', seg))
        P.wmark('vector', 'maskt')
        for j in range(CPS):
            g = 16 + (seg - 2) * CPS + j
            P.wguard('tensor', 'B0', 'B1', 'B2', 'B3')
            for h in range(H):
                par = (h % 2) * 64
                fc = h // 2
                kT = ab[par:par + 64, KT_O + fc * 1024 + j * 128:
                        KT_O + fc * 1024 + j * 128 + 128]
                qT = ab[par:par + 64, QT_O + fc * 1024 + j * 128:
                        QT_O + fc * 1024 + j * 128 + 128]
                bk = scb(h)
                po = scpos(h) * 128
                first = not any(scb(hh) == bk for hh in range(h))
                P.on('tensor', lambda t, kT=kT, qT=qT, bk=bk, po=po, first=first:
                     t.matmul(G[bk][:, po:po + 128], kT, qT, start=first, stop=True))
            P.on('tensor', lambda t: t.drain(), inc='tp')
            P.mark(('sc', g), 'tp')
            P.wguard('tensor', 'SPB')
            for h in range(H):
                kt = ab[:, KTM_O + j * 1024 + h * 64: KTM_O + j * 1024 + h * 64 + 64]
                vt = ab[:, VTM_O + j * 1024 + h * 64: VTM_O + j * 1024 + h * 64 + 64]
                sbk = SP[h % 2]
                off = (h // 2) * 64
                par = (h % 2) * 64
                P.on('tensor', lambda t, kt=kt, vt=vt, sbk=sbk, off=off, par=par:
                     t.matmul(sbk[par:par + 64, off:off + 64], kt, vt,
                              start=False, stop=False, tile_position=(0, par)))
            P.on('tensor', lambda t: t.drain(), inc='tp')
            P.mark(('state', g), 'tp')
            P.wmark('vector', ('sc', g))
            for bk in range(4):
                P.on('vector', lambda v, bk=bk:
                     v.tensor_tensor(ab[:, MSC_O + bk * 512: MSC_O + (bk + 1) * 512],
                                     G[bk][:], sm['maskt'][:], ALU.mult), inc='vp')
                P.setguard(f'B{bk}', 'vp')
            P.mark(('msc', g), 'vp')
            P.wmark('tensor', ('msc', g))
            P.wmark('tensor', ('sevac', g - 1))
            for hh in range(2):
                P.wguard('tensor', 'B4', 'B5')
                for h in range(hh * 8, hh * 8 + 8):
                    ob = 4 + (h % 2)
                    par = (h % 2) * 64
                    po = ((h % 8) // 2) * 128
                    vt = ab[:, VTM_O + j * 1024 + h * 64: VTM_O + j * 1024 + h * 64 + 64]
                    ms = ab[:, MSC_O + scb(h) * 512 + scpos(h) * 128:
                            MSC_O + scb(h) * 512 + scpos(h) * 128 + 128]
                    first = (h - hh * 8) < 2
                    P.on('tensor', lambda t, vt=vt, ms=ms, ob=ob, par=par, po=po,
                         first=first:
                         t.matmul(G[ob][par:par + 64, po:po + 128], vt, ms,
                                  start=first, stop=False, tile_position=(0, par)))
                    Ssl = sm['S'][par:par + 64, (h // 2) * 64: (h // 2) * 64 + 64]
                    fc = h // 2
                    qT = ab[par:par + 64, QT_O + fc * 1024 + j * 128:
                            QT_O + fc * 1024 + j * 128 + 128]
                    P.on('tensor', lambda t, Ssl=Ssl, qT=qT, ob=ob, par=par, po=po:
                         t.matmul(G[ob][par:par + 64, po:po + 128], Ssl, qT,
                                  start=False, stop=True, tile_position=(par, par)))
                P.on('tensor', lambda t: t.drain(), inc='tp')
                P.mark(('omm', g, hh), 'tp')
                P.wmark('scalar', ('omm', g, hh))
                src_e = G[4][0:64, 0:512].rearrange("p (c r) -> p c r", c=4)
                dst_e = packed_dst(PW_O + 8192 + hh * 4 * 1024, j)[0:64]
                P.on('scalar', lambda s, src_e=src_e, dst_e=dst_e:
                     s.copy(dst_e, src_e), inc='sp')
                src_o2 = G[5][64:128, 0:512].rearrange("p (c r) -> p c r", c=4)
                dst_o2 = packed_dst(PW_O + 8192 + hh * 4 * 1024, j)[64:128]
                P.on('scalar', lambda s, src_o2=src_o2, dst_o2=dst_o2:
                     s.copy(dst_o2, src_o2), inc='sp')
                P.setguard('B4', 'sp')
                P.setguard('B5', 'sp')
            P.mark(('omm', g), 'tp')
            P.mark(('oevac', g), 'sp')
            P.wmark('scalar', ('omm', g))
            P.on('scalar', lambda s: s.copy(sm['S'][0:64, :], SP[0][0:64, 0:512]),
                 inc='sp')
            P.on('scalar', lambda s: s.copy(sm['S'][64:128, :], SP[1][64:128, 0:512]),
                 inc='sp')
            P.mark(('sevac', g), 'sp')
            P.setguard('SPB', 'sp')
        P.mark(('attn-done', seg), 'sp')

        P.wmark('tensor', ('attn-done', seg))
        P.wmark('tensor', ('wo', seg))
        for j in range(CPS):
            for dg in range(2):
                yb = (j * 2 + dg) % 2
                P.wguard('tensor', f'B{yb}')
                for fc in range(DC):
                    aoT = ab[:, PW_O + 8192 + fc * 1024 + j * 128:
                             PW_O + 8192 + fc * 1024 + j * 128 + 128]
                    P.on('tensor', lambda t, aoT=aoT, fc=fc, dg=dg, yb=yb:
                         t.matmul(G[yb][:], aoT, pw(0, fc * 1024 + dg * 512, 512),
                                  start=(fc == 0), stop=(fc == DC - 1)))
                P.on('tensor', lambda t: t.drain(), inc='tp')
                P.mark(('wo-mm', seg, j, dg), 'tp')
                P.wmark('scalar', ('wo-mm', seg, j, dg))
                P.wguard('scalar', f'stg{j % 2}')
                stg = ab[:, STG_O + (j % 2) * 1024 + dg * 512:
                         STG_O + (j % 2) * 1024 + dg * 512 + 512]
                P.on('scalar', lambda s, stg=stg, yb=yb: s.copy(stg, G[yb][:]),
                     inc='sp')
                P.setguard(f'B{yb}', 'sp')
            P.mark(('s-stage', seg, j), 'sp')
            P.wmark('sync', ('s-stage', seg, j))
            row = (seg - 2) * 1024 + j * 128
            P.on('sync', lambda s, row=row, j=j: s.dma_start(
                out=A['d_sscr'][row:row + 128, :],
                in_=ab[:, STG_O + (j % 2) * 1024: STG_O + (j % 2) * 1024 + 1024]),
                inc=f'st{j % 2}', n=16)
            P.mark(('s-store', seg, j), f'st{j % 2}')
            P.setguard(f'stg{j % 2}', f'st{j % 2}')
        P.mark(('wo-done', seg), 'tp')
        P.setguard('pw0', 'tp')
        P.setguard('pw1', 'tp')

        if KDBG2 and seg == 3:
            P.wmark('sync', ('wo-done', seg))
            for nm3, off3 in (('dkT', KT_O), ('dqT', QT_O), ('dktm', KTM_O),
                              ('dvtm', VTM_O)):
                for ci in range(8):
                    P.on('sync', lambda s, nm3=nm3, off3=off3, ci=ci: s.dma_start(
                        out=A['d_' + nm3][:, ci * 1024:(ci + 1) * 1024],
                        in_=ab[:, off3 + ci * 1024: off3 + (ci + 1) * 1024]),
                        inc='xc', n=16)
            P.on('sync', lambda s: s.dma_start(out=A['d_dS2'], in_=sm['S'][:]),
                 inc='xc', n=16)
            P.mark(('kdbg2', seg), 'xc')
        for rname in ('w1s0', 'w1s1', 'w3s0', 'w3s1', 'w2h0', 'w2h1'):
            P.guard[rname] = P.marks[('wo-done', seg)]
            if KDBG2 and seg == 3:
                P.guard[rname] = P.marks[('kdbg2', seg)]
        if KDBG2 and seg == 3:
            P.wmark('vector', ('kdbg2', seg))
            P.wmark('gpsimd', ('kdbg2', seg))
        w1s = [ab[:, W13_O + i * 4096: W13_O + (i + 1) * 4096] for i in range(2)]
        w3s = [ab[:, W13_O + 8192 + i * 4096: W13_O + 8192 + (i + 1) * 4096]
               for i in range(2)]
        w2h = [ab[:, W2_O + i * 11264: W2_O + (i + 1) * 11264] for i in range(2)]

        def w13_dma(e2, gi):
            hf0, ntile = GRP[gi]
            sl = gi % 2
            cols = ntile * 128
            for (mat, dram, slots) in (('w1', A['d_w1t'], w1s),
                                       ('w3', A['d_w3t'], w3s)):
                P.wguard('gpsimd', f'{mat}s{sl}')
                dst = slots[sl].rearrange("p (c f) -> p c f", f=512)[:, :, 0:cols]
                src = dram[e2, :, hf0 * 128: hf0 * 128 + cols].rearrange(
                    "(c p) f -> p c f", p=128)
                P.on('gpsimd', lambda g2, dst=dst, src=src:
                     g2.dma_start(out=dst, in_=src), inc=f'{mat}s{sl}', n=16)
                P.mark(('wldr', seg, e2, gi, mat), f'{mat}s{sl}')

        for e in range(E):
            w13_dma(e, 0)
            w13_dma(e, 1)
            for dg in range(2):
                P.wguard('gpsimd', f'w2h{dg}')
                dst = w2h[dg].rearrange("p (c f) -> p c f", f=512)
                src = A['d_w2t'][e, :, dg * 512:(dg + 1) * 512].rearrange(
                    "(c p) f -> p c f", p=128)
                P.on('gpsimd', lambda g2, dst=dst, src=src:
                     g2.dma_start(out=dst, in_=src), inc=f'w2h{dg}', n=16)
                P.mark(('w2ldr', seg, e, dg), f'w2h{dg}')

            if e == 0:
                P.wmark('vector', ('s-store', seg, CPS - 1))
            if KDBG3 and seg == 2 and e == 1:
                P.wmark('sync', ('htt', seg, 0, 21))
                for ci in range(22):
                    P.on('sync', lambda s, ci=ci: s.dma_start(
                        out=A['d_dh'][:, ci * 1024:(ci + 1) * 1024],
                        in_=ab[:, H_O + ci * 1024: H_O + (ci + 1) * 1024]),
                        inc='xc', n=16)
                P.mark(('kdbg3',), 'xc')
                P.wmark('scalar', ('kdbg3',))
                P.wmark('vector', ('kdbg3',))
                P.wmark('tensor', ('kdbg3',))
            for gi, (hf0, ntile) in enumerate(GRP):
                sl = gi % 2
                P.wmark('tensor', ('wldr', seg, e, gi, 'w1'))
                P.wmark('tensor', ('wldr', seg, e, gi, 'w3'))
                for k in range(ntile):
                    hf = hf0 + k
                    ga = 0 if hf % 2 == 0 else 4
                    P.wguard('tensor', f'B{ga}', f'B{ga + 1}')
                    for c in range(DC):
                        lt = w1s[sl][:, c * 512 + k * 128: c * 512 + k * 128 + 128]
                        for sub in range(2):
                            P.on('tensor', lambda t, lt=lt, c=c, sub=sub, ga=ga:
                                 t.matmul(G[ga + sub][:], lt,
                                          ab[:, XNT_O + c * 1024 + sub * 512:
                                             XNT_O + c * 1024 + sub * 512 + 512],
                                          start=(c == 0), stop=(c == DC - 1)))
                    P.wguard('tensor', 'B2', 'B3')
                    for c in range(DC):
                        lt = w3s[sl][:, c * 512 + k * 128: c * 512 + k * 128 + 128]
                        for sub in range(2):
                            P.on('tensor', lambda t, lt=lt, c=c, sub=sub:
                                 t.matmul(G[2 + sub][:], lt,
                                          ab[:, XNT_O + c * 1024 + sub * 512:
                                             XNT_O + c * 1024 + sub * 512 + 512],
                                          start=(c == 0), stop=(c == DC - 1)))
                    P.on('tensor', lambda t: t.drain(), inc='tp')
                    P.mark(('b-mm', seg, e, hf), 'tp')
                    P.wmark('scalar', ('b-mm', seg, e, hf))
                    P.wguard('scalar', 'asil')
                    for sub in range(2):
                        P.on('scalar', lambda s, sub=sub, ga=ga:
                             s.activation(af[:, ASF_O + sub * 512: ASF_O + sub * 512 + 512],
                                          G[ga + sub][:], AF.Silu),
                             inc='sp')
                        P.setguard(f'B{ga + sub}', 'sp')
                    P.mark(('silu', seg, e, hf), 'sp')
                    P.wmark('vector', ('silu', seg, e, hf))
                    for sub in range(2):
                        P.on('vector', lambda v, sub=sub, hf=hf:
                             v.tensor_tensor(ab[:, H_O + hf * 1024 + sub * 512:
                                                H_O + hf * 1024 + sub * 512 + 512],
                                             af[:, ASF_O + sub * 512:
                                                ASF_O + sub * 512 + 512],
                                             G[2 + sub][:], ALU.mult),
                             inc='vp' if sub == 1 else None)
                        P.setguard(f'B{2 + sub}', 'vp')
                    P.mark(('htt', seg, e, hf), 'vp')
                    P.setguard('asil', 'vp')
                P.setguard(f'w1s{sl}', 'tp')
                P.setguard(f'w3s{sl}', 'tp')
                if gi + 2 < len(GRP):
                    w13_dma(e, gi + 2)

            P.wmark('tensor', ('htt', seg, e, 21))
            P.wmark('tensor', ('w2ldr', seg, e, 0))
            P.wmark('tensor', ('w2ldr', seg, e, 1))
            for jj in range(CPS):
                for dg in range(2):
                    ti = jj * 2 + dg
                    yb = 4 + ti % 2
                    P.wguard('tensor', f'B{yb}')
                    for hf in range(NHFT):
                        hp = ab[:, H_O + hf * 1024 + jj * 128:
                                H_O + hf * 1024 + jj * 128 + 128]
                        P.on('tensor', lambda t, hp=hp, hf=hf, dg=dg, yb=yb:
                             t.matmul(G[yb][:], hp, w2h[dg][:, hf * 512:(hf + 1) * 512],
                                      start=(hf == 0), stop=(hf == NHFT - 1)))
                    P.on('tensor', lambda t: t.drain(), inc='tp')
                    P.mark(('y-mm', seg, e, ti), 'tp')
                    P.wmark('vector', ('y-mm', seg, e, ti))
                    ysl = af[:, Y_O + jj * 1024 + dg * 512: Y_O + jj * 1024 + dg * 512 + 512]
                    cwsl = sm['cw'][:, jj * 8 + e: jj * 8 + e + 1]
                    if e == 0:
                        P.on('vector', lambda v, ysl=ysl, yb=yb, cwsl=cwsl:
                             v.tensor_scalar_mul(ysl, G[yb][:], cwsl), inc='vp')
                    else:
                        P.on('vector', lambda v, ysl=ysl, yb=yb, cwsl=cwsl:
                             v.scalar_tensor_tensor(ysl, G[yb][:], cwsl, ysl,
                                                    ALU.mult, ALU.add), inc='vp')
                    P.setguard(f'B{yb}', 'vp')
                    P.mark(('stt', seg, e, ti), 'vp')
            P.setguard('w2h0', 'tp')
            P.setguard('w2h1', 'tp')
        P.mark(('w2-done', seg), 'tp')
        P.setguard('pw0', 'tp')
        P.setguard('pw1', 'tp')
        P.setguard('pw2', 'tp')

        for jj in range(CPS):
            row = (seg - 2) * 1024 + jj * 128
            P.wmark('gpsimd', ('s-store', seg, jj))
            P.wguard('gpsimd', f'sld{jj % 2}')
            sld = af[:, SLD_O + (jj % 2) * 1024: SLD_O + (jj % 2) * 1024 + 1024]
            P.on('gpsimd', lambda g2, sld=sld, row=row: g2.dma_start(
                out=sld, in_=A['d_sscr'][row:row + 128, :]), inc=f'sld{jj % 2}', n=16)
            P.mark(('s-ld', seg, jj), f'sld{jj % 2}')
            xr = af[:, XR_O + (jj % 2) * 1024: XR_O + (jj % 2) * 1024 + 1024]
            P.wguard('sync', f'xr{jj % 2}')
            P.on('sync', lambda s, xr=xr, row=row: s.dma_start(
                out=xr, in_=A['d_xseq'][2048 + row: 2048 + row + 128, :]),
                inc=f'xr{jj % 2}', n=16)
            P.mark(('comb-x', seg, jj), f'xr{jj % 2}')
            for dg in range(2):
                ysl = af[:, Y_O + jj * 1024 + dg * 512: Y_O + jj * 1024 + dg * 512 + 512]
                P.wmark('scalar', ('stt', seg, E - 1, jj * 2 + dg))
                P.wguard('scalar', 'gs')
                gsc = af[:, GS_O + dg * 512: GS_O + dg * 512 + 512]
                P.on('scalar', lambda s, gsc=gsc, ysl=ysl:
                     s.activation(gsc, ysl, AF.Silu), inc='sp')
                P.mark(('csilu', seg, jj, dg), 'sp')
                P.wmark('vector', ('csilu', seg, jj, dg))
                P.wmark('vector', ('s-ld', seg, jj))
                P.wmark('vector', ('comb-x', seg, jj))
                sgs = af[:, SLD_O + (jj % 2) * 1024 + dg * 512:
                         SLD_O + (jj % 2) * 1024 + dg * 512 + 512]
                xsl = af[:, XR_O + (jj % 2) * 1024 + dg * 512:
                         XR_O + (jj % 2) * 1024 + dg * 512 + 512]
                P.on('vector', lambda v, gsc=gsc, sgs=sgs: v.tensor_mul(gsc, gsc, sgs))
                P.on('vector', lambda v, xsl=xsl, gsc=gsc: v.tensor_add(xsl, gsc, xsl),
                     inc='vp')
            P.setguard('gs', 'vp')
            P.setguard(f'sld{jj % 2}', 'vp')
            P.mark(('comb-v', seg, jj), 'vp')
            P.wmark('sync', ('comb-v', seg, jj))
            P.on('sync', lambda s, xr=xr, row=row: s.dma_start(
                out=A['d_out'][row:row + 128, :], in_=xr), inc=f'ost{jj % 2}', n=16)
            P.mark(('out-store', seg, jj), f'ost{jj % 2}')
            P.setguard(f'xr{jj % 2}', f'ost{jj % 2}')

    if KDBG:
        for s2 in ('tp', 'vp', 'sp', 'sld0', 'sld1'):
            P.w('sync', s2, P.cnt[s2])
        for ci in range(32):
            P.on('sync', lambda s, ci=ci: s.dma_start(
                out=A['d_dab'][:, ci * 1024:(ci + 1) * 1024],
                in_=ab[:, ci * 1024:(ci + 1) * 1024]), inc='xc', n=16)
        for ci in range(8):
            P.on('sync', lambda s, ci=ci: s.dma_start(
                out=A['d_daf'][:, ci * 1024:(ci + 1) * 1024],
                in_=af[:, ci * 1024:(ci + 1) * 1024]), inc='xc', n=16)
        P.on('sync', lambda s: s.dma_start(out=A['d_dsm'], in_=sm['S'][:]), inc='xc', n=16)
        P.on('sync', lambda s: s.dma_start(out=A['d_dcw'], in_=sm['cw'][:]), inc='xc', n=16)
        P.on('sync', lambda s: s.dma_start(out=A['d_dbs'], in_=sm['bseg'][:]), inc='xc', n=16)
        P.on('sync', lambda s: s.dma_start(out=A['d_dqn'], in_=sm['qn'][:]), inc='xc', n=16)
    for s2 in ('st0', 'st1', 'ost0', 'ost1'):
        P.w('sync', s2, P.cnt[s2])


def build_nc():
    nc = bass.Bass("TRN2", target_bir_lowering=False)
    A = {}
    A['d_xseq'] = nc.dram_tensor("xseq", [VIEW, D], F32, kind="ExternalInput").ap()
    for nmW in ('wq', 'wk', 'wv', 'wo'):
        A['d_' + nmW] = nc.dram_tensor(nmW, [D, D], F32, kind="ExternalInput").ap()
    A['d_wb'] = nc.dram_tensor("wb", [D, H], F32, kind="ExternalInput").ap()
    A['d_wgt'] = nc.dram_tensor("wgt", [D, E], F32, kind="ExternalInput").ap()
    A['d_bbb'] = nc.dram_tensor("bbb", [128, H], F32, kind="ExternalInput").ap()
    A['d_w1t'] = nc.dram_tensor("w1t", [E, D, HF], F32, kind="ExternalInput").ap()
    A['d_w3t'] = nc.dram_tensor("w3t", [E, D, HF], F32, kind="ExternalInput").ap()
    A['d_w2t'] = nc.dram_tensor("w2t", [E, HF, D], F32, kind="ExternalInput").ap()
    A['d_maskt'] = nc.dram_tensor("maskt", [128, 512], F32, kind="ExternalInput").ap()
    A['d_ident'] = nc.dram_tensor("identin", [128, 128], F32, kind="ExternalInput").ap()
    A['d_out'] = nc.dram_tensor("out", [2048, D], F32, kind="ExternalOutput").ap()
    A['d_sscr'] = nc.dram_tensor(
        "sscr", [2048, D], BF16,
        kind="ExternalOutput" if (KDBG or KDBG2) else "Internal").ap()
    if KDBG:
        A['d_dab'] = nc.dram_tensor("dab", [128, 32768], BF16, kind="ExternalOutput").ap()
        A['d_daf'] = nc.dram_tensor("daf", [128, 8192], F32, kind="ExternalOutput").ap()
        A['d_dsm'] = nc.dram_tensor("dsm", [128, 512], BF16, kind="ExternalOutput").ap()
        A['d_dcw'] = nc.dram_tensor("dcw", [128, 64], F32, kind="ExternalOutput").ap()
        A['d_dbs'] = nc.dram_tensor("dbs", [128, 128], F32, kind="ExternalOutput").ap()
        A['d_dqn'] = nc.dram_tensor("dqn", [128, 112], F32, kind="ExternalOutput").ap()
    if KDBG3:
        A['d_dh'] = nc.dram_tensor("dh", [128, 22528], BF16, kind="ExternalOutput").ap()
    if KDBG2:
        for nm3 in ('dkT', 'dqT', 'dktm', 'dvtm'):
            A['d_' + nm3] = nc.dram_tensor(nm3, [128, 8192], BF16, kind="ExternalOutput").ap()
        A['d_dS2'] = nc.dram_tensor("dS2", [128, 512], BF16, kind="ExternalOutput").ap()

    ctx = ExitStack()
    ab = ctx.enter_context(nc.sbuf_tensor("ab", [128, AB_COLS], BF16))
    af = ctx.enter_context(nc.sbuf_tensor("af", [128, AF_COLS], F32))
    smh = {}
    for nm2, shape, dt in (('S', [128, 512], BF16), ('maskt', [128, 512], F32),
                           ('ident', [128, 128], BF16), ('bbb', [128, 16], F32),
                           ('wb', [128, 128], BF16), ('wg', [128, 64], BF16),
                           ('stats', [128, 32], F32), ('btmp', [128, 128], F32),
                           ('bseg', [128, 128], F32), ('rtr', [128, 8], F32),
                           ('ex', [128, 48], F32), ('cw', [128, 64], F32),
                           ('qn', [128, 112], F32)):
        smh[nm2] = ctx.enter_context(nc.sbuf_tensor(nm2 + "_sb", shape, dt))
    SPt = [ctx.enter_context(nc.psum_tensor(f"SP{i}", [128, 512], F32)) for i in range(2)]
    Gt = [ctx.enter_context(nc.psum_tensor(f"G{i}", [128, 512], F32)) for i in range(6)]
    sems = {s: ctx.enter_context(nc.semaphore(s)) for s in
            ('wc', 'xc', 'pw0', 'pw1', 'pw2', 'w1s0', 'w1s1', 'w3s0', 'w3s1',
             'w2h0', 'w2h1', 'sld0', 'sld1', 'xr0', 'xr1', 'st0', 'st1',
             'ost0', 'ost1', 'tp', 'vp', 'sp')}

    A['ab'], A['af'] = ab[:], af[:]
    A['sm'] = {k: v[:] for k, v in smh.items()}
    A['G'] = [g[:] for g in Gt]
    A['SP'] = [s[:] for s in SPt]
    A['Gb'] = [A['G'][i].bitcast(BF16) for i in range(2)]

    shared_marks = {}
    passes = {}
    with nc.Block() as blk:
        for which, deco in (('tensor', blk.tensor), ('vector', blk.vector),
                            ('scalar', blk.scalar), ('gpsimd', blk.gpsimd),
                            ('sync', blk.sync)):
            Pp = Pass(nc, which, sems, shared_marks)
            passes[which] = Pp
            deco(lambda eng, Pp=Pp: emit(Pp, A))
    ref = passes['tensor'].cnt
    for w2_, p2 in passes.items():
        assert p2.cnt == ref, (w2_, p2.cnt, ref)
    ctx.close()
    return nc


def kernel(L0_attn_x, L0_attn_Wq, L0_attn_Wk, L0_attn_Wv, L0_attn_Wo,
           L0_attn_Wb, L0_attn_bb, L0_ffn_Wg, L0_ffn_W1, L0_ffn_W2, L0_ffn_W3):
    x = np.ascontiguousarray(np.asarray(L0_attn_x, dtype=np.float32))
    wq = np.ascontiguousarray(np.asarray(L0_attn_Wq, dtype=np.float32))
    wk = np.ascontiguousarray(np.asarray(L0_attn_Wk, dtype=np.float32))
    wv = np.ascontiguousarray(np.asarray(L0_attn_Wv, dtype=np.float32))
    wo = np.ascontiguousarray(np.asarray(L0_attn_Wo, dtype=np.float32))
    wb = np.ascontiguousarray(np.asarray(L0_attn_Wb, dtype=np.float32))
    bb = np.asarray(L0_attn_bb, dtype=np.float32)
    wgt = np.ascontiguousarray(np.asarray(L0_ffn_Wg, dtype=np.float32).T)
    w1t = np.ascontiguousarray(np.asarray(L0_ffn_W1, dtype=np.float32).transpose(0, 2, 1))
    w3t = np.ascontiguousarray(np.asarray(L0_ffn_W3, dtype=np.float32).transpose(0, 2, 1))
    w2t = np.ascontiguousarray(np.asarray(L0_ffn_W2, dtype=np.float32).transpose(0, 2, 1))
    bbb = np.ascontiguousarray(np.broadcast_to(bb[None, :], (128, H)))
    maskt = np.ascontiguousarray(
        np.tile(np.triu(np.ones((128, 128), dtype=np.float32)), (1, 4)))
    ident = np.eye(128, dtype=np.float32)

    zeros = np.zeros((2048, D), dtype=np.float32)
    in_maps = []
    for core in range(8):
        bidx, half = core // 2, core % 2
        if half == 0:
            xseq = np.ascontiguousarray(
                np.concatenate([zeros, x[bidx, 0:2048]], axis=0))
        else:
            xseq = np.ascontiguousarray(x[bidx])
        in_maps.append({
            "xseq": xseq, "wq": wq, "wk": wk, "wv": wv, "wo": wo, "wb": wb,
            "wgt": wgt, "bbb": bbb, "w1t": w1t, "w3t": w3t, "w2t": w2t,
            "maskt": maskt, "identin": ident,
        })

    nc = build_nc()
    res = run_bass_kernel_spmd(nc, in_maps, core_ids=list(range(8)))
    out = np.empty((B, L, D), dtype=np.float32)
    for core in range(8):
        bidx, half = core // 2, core % 2
        out[bidx, half * 2048:(half + 1) * 2048] = res.results[core]["out"]
    if KDBG or KDBG2 or KDBG3:
        import pickle
        with open('/tmp/kdbg.pkl', 'wb') as fh:
            pickle.dump([{k: np.asarray(v) for k, v in r.items()}
                         for r in res.results], fh)
    return out
